# version 17
# baseline (speedup 1.0000x reference)
import contextlib
import os
import numpy as np
import ml_dtypes

import concourse.bass as bass
import concourse.mybir as mybir
from concourse.bass import IndirectOffsetOnAxis
from concourse.bass_utils import run_bass_kernel_spmd

F32 = mybir.dt.float32
BF16 = mybir.dt.bfloat16
I32 = mybir.dt.int32
AF = mybir.ActivationFunctionType
ALU = mybir.AluOpType
AXX = mybir.AxisListType.X

N_CORES = 8
SEQ = 8192
D = 1024
H = 16
FD = 256
DFF = 4096
NEXP = 8
TG_A = 512
TG_B = 256
TG_C = 256

STREAMS = ("pe", "act", "dve", "pool", "sp")


class Plan:

    DMA_SLOTS = 8

    def __init__(self):
        self.steps = []
        self.by_stream = {s: [] for s in STREAMS}
        self.dma_seq = {s: [] for s in STREAMS}

    def add(self, stream, fn, deps=(), dma=False, tag=""):
        sid = len(self.steps)
        deps = tuple(d for d in deps if d is not None)
        if dma:
            qi = len(self.dma_seq[stream])
            sem_key = f"{stream}.dma{qi % self.DMA_SLOTS}"
            if qi >= self.DMA_SLOTS:
                deps = deps + (self.dma_seq[stream][qi - self.DMA_SLOTS],)
            self.dma_seq[stream].append(sid)
        else:
            sem_key = stream
        self.steps.append((stream, sem_key, 16 if dma else 1, fn, deps, tag))
        self.by_stream[stream].append(sid)
        return sid

    def emit(self, nc, block):
        n = len(self.steps)
        needed = [False] * n
        for (_, _, _, _, deps, _) in self.steps:
            for d in deps:
                needed[d] = True
        for sid, (_, key, _, _, _, _) in enumerate(self.steps):
            if ".dma" in key:
                needed[sid] = True
        counts = {}
        count_after = [None] * n
        for sid, (_, key, inc, _, _, _) in enumerate(self.steps):
            if needed[sid]:
                counts[key] = counts.get(key, 0) + inc
                count_after[sid] = counts[key]
        ctx = contextlib.ExitStack()
        sems = {k: ctx.enter_context(nc.semaphore("s_" + k.replace(".", "_")))
                for k in sorted(counts)}

        def make_body(stream):
            def body(eng):
                waited = {}
                for sid in self.by_stream[stream]:
                    (_, skey, inc, fn, deps, tag) = self.steps[sid]
                    for d in deps:
                        dkey = self.steps[d][1]
                        ca = count_after[d]
                        if waited.get(dkey, 0) < ca:
                            eng.wait_ge(sems[dkey], ca)
                            waited[dkey] = ca
                    ins = fn(eng)
                    if needed[sid]:
                        assert ins is not None, f"step {sid} [{tag}] returned None"
                        ins.then_inc(sems[skey], inc)
            return body

        block.sync(make_body("sp"))
        block.scalar(make_body("act"))
        block.vector(make_body("dve"))
        block.tensor(make_body("pe"))
        block.gpsimd(make_body("pool"))
        return ctx


def build_program(stage=3):
    nc = bass.Bass("TRN2", target_bir_lowering=False)

    x_d = nc.dram_tensor("x", [SEQ, D], F32, kind="ExternalInput")
    wq_d = nc.dram_tensor("wq", [D, D], F32, kind="ExternalInput")
    wk_d = nc.dram_tensor("wk", [D, D], F32, kind="ExternalInput")
    wv_d = nc.dram_tensor("wv", [D, D], F32, kind="ExternalInput")
    wp_d = nc.dram_tensor("wp", [D, D], F32, kind="ExternalInput")
    bq_d = nc.dram_tensor("bq", [D], F32, kind="ExternalInput")
    bk_d = nc.dram_tensor("bk", [D], F32, kind="ExternalInput")
    bvr_d = nc.dram_tensor("bvrep", [128, H * 65], F32, kind="ExternalInput")
    bp_d = nc.dram_tensor("bproj", [1, D], F32, kind="ExternalInput")
    wqf_d = nc.dram_tensor("wqf", [128, FD], F32, kind="ExternalInput")
    wkf_d = nc.dram_tensor("wkf", [128, FD], F32, kind="ExternalInput")
    wrt_d = nc.dram_tensor("wrt", [NEXP, D], F32, kind="ExternalInput")
    br_d = nc.dram_tensor("br", [1, NEXP], F32, kind="ExternalInput")
    w1_d = nc.dram_tensor("w1", [NEXP * D, DFF], F32, kind="ExternalInput")
    b1_d = nc.dram_tensor("b1", [NEXP * 32, 128], F32, kind="ExternalInput")
    w2_d = nc.dram_tensor("w2", [NEXP * DFF, D], F32, kind="ExternalInput")
    b2_d = nc.dram_tensor("b2", [NEXP, D], F32, kind="ExternalInput")
    eye_d = nc.dram_tensor("eyebf", [128, 128], BF16, kind="ExternalInput")
    eyef_d = nc.dram_tensor("eyef", [128, 128], F32, kind="ExternalInput")
    iota_d = nc.dram_tensor("iotap", [128, 1], F32, kind="ExternalInput")
    rowc_d = nc.dram_tensor("rowc", [1, 16], F32, kind="ExternalInput")
    out_d = nc.dram_tensor("out", [SEQ, D], F32, kind="ExternalOutput")
    x1_scr = nc.dram_tensor("x1scr", [SEQ, D], F32, kind="Internal")
    yt_scr = nc.dram_tensor("ytscr", [D, SEQ], BF16, kind="Internal")

    ctx = contextlib.ExitStack()

    def sb(name, shape, dt):
        return ctx.enter_context(nc.sbuf_tensor(name, shape, dt))

    selw1 = sb("selw1", [128, 32768], BF16)
    selw2 = sb("selw2", [128, 32768], BF16)
    warena = sb("warena", [128, 16384], BF16)
    a3 = sb("a3", [128, 13376], BF16)
    d4 = sb("d4", [128, 4096], BF16)

    ctxT = sb("ctxT", [128, H * 2 * 65], BF16)
    wqf = sb("wqf_s", [128, FD], BF16)
    wkf = sb("wkf_s", [128, FD], BF16)
    eye = sb("eye", [128, 128], BF16)
    eyef = sb("eyef_s", [128, 128], F32)
    bkc = sb("bkc", [128, 8], F32)
    bprow = sb("bprow_s", [1, D], BF16)
    bqrow = sb("bqrow", [1, D], BF16)
    ones_row = sb("ones_row", [1, 512], BF16)
    ones_col = sb("ones_col", [128, 1], BF16)
    iotap = sb("iotap_s", [128, 1], F32)
    rowc = sb("rowc_s", [1, 16], F32)
    b2row = ctxT[0:1, 0:D]
    dsc = sb("dsc", [128, 64], F32)
    b1selT = sb("b1selT", [128, 32], F32)
    idx1 = sb("idx1_s", [128, 1], I32)
    idx2 = sb("idx2_s", [128, 1], I32)
    idxb1 = sb("idxb1", [32, 1], I32)
    idxb2 = sb("idxb2", [2, 1], I32)
    idxf = sb("idxf", [128, 4], F32)
    erep = sb("erep", [128, 1], F32)
    ebf = sb("ebf", [1, 1], BF16)
    cfix = sb("cfix", [128, 130], F32)
    ones8f = sb("ones8f", [1, 8], F32)
    LNS = []
    for si in range(2):
        LNS.append({k: sb(f"ln{k}{si}", [128, 4], F32)
                    for k in ("sum", "sq", "mean", "var", "var2", "rstd", "nmr")})
    epscol = sb("epscol", [128, 1], F32)

    def fv(t, off, p, n):
        return t[0:p, off // 2: off // 2 + 2 * n].bitcast(F32)

    def bvw(t, off, p, n):
        return t[0:p, off // 2: off // 2 + n]

    x_tile = [fv(selw2, 0, 128, 4096), fv(selw2, 16384, 128, 4096)]
    x_tileB = [fv(selw2, 8192 * i, 128, 2048) for i in range(4)]
    x1sb_bufs = [fv(selw2, 32768, 128, 2048), fv(selw2, 53248, 128, 2048)]
    ysb_bufs = [bvw(selw2, 40960, 128, 2048), bvw(selw2, 49152, 128, 2048)]
    prodrow = fv(selw2, 45056, 1, 1024)
    w2sel = selw2
    ctxacc = [fv(selw1, 0, 65, H * FD), fv(selw1, 16384, 65, H * FD)]
    rows = fv(selw1, 0, 1, 2048)
    rowbuf = fv(selw1, 8192, 2, 1024)
    wr8_sb = fv(selw1, 16384, 8, 1024)
    prod8 = fv(selw1, 20608, 8, 1040)
    kp_bufs = [bvw(selw1, 53376, 128, 1024), bvw(selw1, 55424, 128, 1024)]
    bvrep = fv(selw1, 32768, 128, H * 65)
    lnjunk = bvw(selw1, 36928, 128, 4096)
    xn_sb = bvw(selw1, 45184, 128, 4096)
    w1sel = selw1
    wk_sb = bvw(warena, 0, 128, 8192)
    wv_sb = bvw(warena, 16384, 128, 8192)
    wq_sb = wk_sb
    wp_sb = wv_sb
    x1_tile = [fv(warena, 0, 128, 2048), fv(warena, 8192, 128, 2048)]
    xnT_A = bvw(a3, 0, 128, 4096)
    kT_A = bvw(a3, 8192, 128, 4096)
    v_A = bvw(a3, 16384, 128, H * 65 * 4)
    kp_A = bvw(a3, 24704, 128, 1024)
    xnT_B = bvw(a3, 0, 128, 2048)
    qT_B = bvw(a3, 4096, 128, 2048)
    qpT_B = [bvw(a3, 8192, 128, 512), bvw(a3, 9216, 128, 512)]
    atok_B = [bvw(a3, 10240, 128, 1024), bvw(a3, 12288, 128, 1024)]
    attT_B = [bvw(a3, 14336, 128, 1024), bvw(a3, 16384, 128, 1024)]
    h_C = bvw(a3, 0, 128, 8192)
    yT_C = [bvw(a3, 16384, 128, 2048), bvw(a3, 20480, 128, 2048)]
    ytout_B = [bvw(d4, 0, 128, 2048), bvw(d4, 4096, 128, 2048)]
    out_C = fv(d4, 0, 128, 2048)

    PX = ctx.enter_context(nc.psum_tensor("px", [128, 1024], F32))
    PV = ctx.enter_context(nc.psum_tensor("pv", [128, 1024], F32))
    PK = ctx.enter_context(nc.psum_tensor("pk", [128, 1024], F32))
    PC = ctx.enter_context(nc.psum_tensor("pc", [128, 512], F32))
    PS = ctx.enter_context(nc.psum_tensor("ps", [128, 512], F32))
    PXb = PX[:, :].bitcast(BF16)

    P = Plan()

    def dma(stream, out, in_, deps=(), tag="", slow=False):
        return P.add(stream,
                     lambda e, out=out, in_=in_: e.dma_start(
                         out=out, in_=in_, allow_slow_non_contiguous=slow),
                     deps=deps, dma=True, tag=tag)

    def mset(stream, ap, val, tag=""):
        return P.add(stream, lambda e, ap=ap, val=val: e.memset(ap, val), tag=tag)

    s_eye = dma("sp", eye[:], eye_d[:, :], tag="eye")
    s_eyef = dma("sp", eyef[:], eyef_d[:, :], tag="eyef")
    s_iota = dma("sp", iotap[:], iota_d[:, :], tag="iota")
    s_rowc = dma("sp", rowc[0:1, :], rowc_d[:, :], tag="rowc")
    s_bk = dma("sp", bkc[:], bk_d.rearrange("(c p) -> p c", p=128), tag="bk", slow=True)
    s_bvr = dma("sp", bvrep, bvr_d[:, :], tag="bvrep")
    s_wqf = dma("pool", wqf[:], wqf_d[:, :], tag="wqf")
    s_wkf = dma("pool", wkf[:], wkf_d[:, :], tag="wkf")
    s_bpr = dma("pool", bprow[0:1, :], bp_d[:, :], tag="bprow")
    s_wk = dma("pool", wk_sb.rearrange("p (dc c) -> p dc c", c=D),
               wk_d.rearrange("(dc p) c -> p dc c", p=128), tag="wk")
    s_wv = dma("pool", wv_sb.rearrange("p (dc c) -> p dc c", c=D),
               wv_d.rearrange("(dc p) c -> p dc c", p=128), tag="wv")
    s_ones = mset("dve", ones_row[0:1, :], 1.0, tag="ones_row")
    s_bqr = dma("pool", bqrow[0:1, :], bq_d.rearrange("d -> 1 d" if False else "(one d) -> one d", one=1), tag="bqrow")
    s_onesc = mset("dve", ones_col[:, :], 1.0, tag="ones_col")
    s_vones = mset("dve", v_A.rearrange("p (s f) -> p s f", f=65)[:, :, 64:65], 1.0,
                   tag="v_ones")
    s_eps = mset("dve", epscol[:], 1e-5, tag="eps")
    s_ones8 = mset("dve", ones8f[0:1, :], 1.0, tag="ones8")
    s_pad8 = mset("dve", prod8[0:8, 1024:1032], 0.0, tag="pad8")

    _ln_chain = [[], []]

    def ln_stats(x_ap, nsub, deps, pfx, si=0):
        lnsum, lnsq, lnmean, lnvar, lnvar2 = (
            LNS[si]["sum"], LNS[si]["sq"], LNS[si]["mean"], LNS[si]["var"],
            LNS[si]["var2"])
        deps = list(deps) + list(_ln_chain[si])
        xv = x_ap.rearrange("p (s d) -> p s d", d=D)
        s_red = P.add("dve", lambda e: e.reduce_sum(lnsum[:, 0:nsub], xv, axis=AXX),
                      deps=deps, tag=pfx + ".red")
        sqs = []
        for s in range(nsub):
            sqs.append(P.add(
                "dve",
                lambda e, s=s: e.scalar_tensor_tensor(
                    lnjunk[:, s * D:(s + 1) * D], x_ap[:, s * D:(s + 1) * D], 1.0,
                    x_ap[:, s * D:(s + 1) * D], ALU.mult, ALU.mult,
                    accum_out=lnsq[:, s:s + 1]),
                deps=list(deps) if s == 0 else [], tag=pfx + f".sq{s}"))
        s_mean = P.add("dve", lambda e: e.tensor_scalar(lnmean[:, 0:nsub], lnsum[:, 0:nsub],
                                                        1.0 / D, None, ALU.mult),
                       deps=[s_red], tag=pfx + ".mean")
        s_v0 = P.add("dve", lambda e: e.tensor_tensor(lnvar[:, 0:nsub], lnmean[:, 0:nsub],
                                                      lnmean[:, 0:nsub], ALU.mult),
                     deps=[s_mean], tag=pfx + ".v0")
        s_v1 = P.add("dve", lambda e: e.tensor_scalar(lnvar2[:, 0:nsub], lnsq[:, 0:nsub],
                                                      1.0 / D, None, ALU.mult),
                     deps=[sqs[-1]], tag=pfx + ".v1")
        s_v2 = P.add("dve", lambda e: e.tensor_tensor(lnvar[:, 0:nsub], lnvar2[:, 0:nsub],
                                                      lnvar[:, 0:nsub], ALU.subtract),
                     deps=[s_v0, s_v1], tag=pfx + ".v2")
        return s_v2, s_red

    def ln_apply(x_ap, nsub, xn_out, pfx, s_v2, si=0, extra=()):
        lnmean, lnvar, lnvar2, lnrstd, lnnmr = (
            LNS[si]["mean"], LNS[si]["var"], LNS[si]["var2"], LNS[si]["rstd"],
            LNS[si]["nmr"])
        s_lg = P.add("act", lambda e: e.activation(lnvar2[:, 0:nsub], lnvar[:, 0:nsub],
                                                   AF.Ln, bias=epscol[:], scale=1.0),
                     deps=[s_v2, s_eps] + list(extra), tag=pfx + ".ln")
        s_rstd = P.add("act", lambda e: e.activation(lnrstd[:, 0:nsub], lnvar2[:, 0:nsub],
                                                     AF.Exp, bias=0.0, scale=-0.5),
                       deps=[s_lg], tag=pfx + ".rstd")
        s_nm0 = P.add("dve", lambda e: e.tensor_tensor(lnvar2[:, 0:nsub], lnmean[:, 0:nsub],
                                                       lnrstd[:, 0:nsub], ALU.mult),
                      deps=[s_rstd], tag=pfx + ".nm0")
        s_nmr = P.add("dve", lambda e: e.tensor_scalar(lnnmr[:, 0:nsub], lnvar2[:, 0:nsub],
                                                       -1.0, None, ALU.mult),
                      deps=[s_nm0], tag=pfx + ".nmr")
        affs = []
        for s in range(nsub):
            affs.append(P.add(
                "dve",
                lambda e, s=s: e.tensor_scalar(xn_out[:, s * D:(s + 1) * D],
                                               x_ap[:, s * D:(s + 1) * D],
                                               lnrstd[:, s:s + 1], lnnmr[:, s:s + 1],
                                               ALU.mult, ALU.add),
                deps=[s_nmr], tag=pfx + f".aff{s}"))
        _ln_chain[si][:] = affs
        return affs

    def ln_steps(x_ap, nsub, xn_out, deps, pfx, si=0):
        s_v2, s_red = ln_stats(x_ap, nsub, deps, pfx, si)
        affs = ln_apply(x_ap, nsub, xn_out, pfx, s_v2, si)
        return affs, s_red

    def tp_steps(src_bf, dst_bf, nsub, T, deps, pfx, extra_guard=None):
        evs = []
        for s in range(nsub):
            half = s % 2
            pb = PXb[:, half * 1024:(half + 1) * 1024]
            if len(evs) >= 2:
                guard = [evs[-2]]
            else:
                guard = [extra_guard] if extra_guard is not None else []
            tps = []
            for c in range(8):
                tps.append(P.add(
                    "pe",
                    lambda e, s=s, c=c, pb=pb: e.transpose(
                        pb[:, c * 128:(c + 1) * 128],
                        src_bf[:, s * D + c * 128: s * D + (c + 1) * 128], eye[:]),
                    deps=(list(deps) + [s_eye] + guard) if c == 0 else [],
                    tag=pfx + f".tp{s}.{c}"))
            evs.append(P.add(
                "dve",
                lambda e, s=s, pb=pb: e.tensor_copy(
                    dst_bf.rearrange("p (c t) -> p c t", t=T)[:, :, s * 128:(s + 1) * 128],
                    pb.rearrange("p (c t) -> p c t", t=128)),
                deps=[tps[-1]], tag=pfx + f".tpev{s}"))
        return evs

    nt_a = SEQ // TG_A
    xbuf_guard = {0: [], 1: []}
    ctx_ev = {}
    last_ctx_ev = [None, None]
    last_exp = {}
    prev_ln = []
    prev_trs = None
    prev_kev = None
    prev_vev = None
    last_kmm = None
    last_vev = None

    def emitA_p1a(t):
        buf = t % 2
        xa = x_tile[buf]
        s_ldx = dma("sp", xa.rearrange("p (s d) -> p s d", d=D),
                    x_d[t * TG_A:(t + 1) * TG_A, :].rearrange("(s p) d -> p s d", p=128),
                    deps=xbuf_guard[buf], tag=f"A{t}.ldx")
        xbuf_guard[buf] = []
        ln_deps = [s_ldx] + prev_ln + ([prev_trs] if prev_trs is not None else [])
        s_v2, s_red = ln_stats(xa, 4, ln_deps, f"A{t}.ln")
        xbuf_guard[buf].append(s_red)
        return xa, s_v2

    def emitA_p1c(t, xa, s_v2):
        nonlocal prev_ln
        affs = ln_apply(xa, 4, xn_sb, f"A{t}.ln", s_v2, extra=[prev_trs])
        prev_ln = affs
        xbuf_guard[t % 2].append(affs[-1])
        return affs

    def emitA_p1b(t, affs):
        nonlocal prev_trs
        trs = tp_steps(xn_sb, xnT_A, 4, TG_A, affs, f"A{t}", extra_guard=prev_kev)
        prev_trs = trs[-1]
        return trs

    def emitA_p2(t, trs):
        nonlocal prev_kev, prev_vev, last_kmm, last_vev
        kevs = []
        kmms = []
        for c in range(8):
            half = c % 2
            pb = PX[:, half * 512:(half + 1) * 512]
            mdeps = [trs[-1], s_wk, s_ones]
            if len(kevs) >= 2:
                mdeps.append(kevs[-2])
            elif prev_kev is not None:
                mdeps.append(prev_kev)

            def kmm(e, c=c, pb=pb):
                ins = None
                wkv = wk_sb.rearrange("p (dc c) -> p dc c", c=D)
                xv = xnT_A.rearrange("p (dc t) -> p dc t", t=TG_A)
                for dc in range(8):
                    ins = e.matmul(pb, wkv[:, dc, c * 128:(c + 1) * 128], xv[:, dc, :],
                                   start=(dc == 0), stop=(dc == 7))
                return ins
            kmms.append(P.add("pe", kmm, deps=mdeps, tag=f"A{t}.kmm{c}"))
            kevs.append(P.add(
                "act",
                lambda e, c=c, pb=pb: e.activation(kT_A[:, c * TG_A:(c + 1) * TG_A], pb,
                                                   AF.Identity, bias=bkc[:, c:c + 1],
                                                   scale=1.0),
                deps=[kmms[-1], s_bk], tag=f"A{t}.kev{c}"))
        last_kmm = kmms[-1]
        prev_kev = kevs[-1]
        vevs = []
        for s in range(4):
            vdeps = [trs[-1], s_wv]
            if vevs:
                vdeps.append(vevs[-1])
            else:
                vdeps.append(s_vones)
                if prev_vev is not None:
                    vdeps.append(prev_vev)
                vdeps.append(last_exp.get(("bank", 1)))

            def vmm(e, s=s):
                ins = None
                xv = xnT_A.rearrange("p (dc t) -> p dc t", t=TG_A)
                wvv = wv_sb.rearrange("p (dc c) -> p dc c", c=D)
                for half in range(2):
                    pb = PV[:, half * 512:(half + 1) * 512]
                    for dc in range(8):
                        ins = e.matmul(pb, xv[:, dc, s * 128:(s + 1) * 128],
                                       wvv[:, dc, half * 512:(half + 1) * 512],
                                       start=(dc == 0), stop=(dc == 7))
                return ins
            s_vmm = P.add("pe", vmm, deps=vdeps, tag=f"A{t}.vmm{s}")

            def vev(e, s=s):
                dv = v_A.rearrange("p (ss hh f) -> p ss hh f", hh=16, f=65)[:, s, :, 0:64]
                e.tensor_copy(dv[:, 0:8, :],
                              PV[:, 0:512].rearrange("p (hh f) -> p hh f", f=64))
                return e.tensor_copy(dv[:, 8:16, :],
                                     PV[:, 512:1024].rearrange("p (hh f) -> p hh f", f=64))
            vevs.append(P.add("dve", vev, deps=[s_vmm], tag=f"A{t}.vev{s}"))
        prev_vev = vevs[-1]
        last_vev = vevs[-1]
        return kevs, vevs

    def emitA_p3(t, kevs, vevs):
        kTv = kT_A.rearrange("p (c t) -> p c t", t=TG_A)
        vv = v_A.rearrange("p (ss hh f) -> p ss hh f", hh=16, f=65)
        exps = {}

        def emit_kq(h):
            par = h % 2
            bank = PK if par == 0 else PV
            kdeps = [kevs[h // 2], s_wkf]
            g = last_exp.get(("bank", par))
            if g is not None:
                kdeps.append(g)
            if par == 1 and h == 1:
                kdeps.append(vevs[-1])

            def kqmm(e, h=h, par=par, bank=bank):
                ins = None
                for s in range(4):
                    ins = e.matmul(bank[:, s * 256:(s + 1) * 256],
                                   kTv[64 * par:64 * par + 64, h // 2,
                                       s * 128:(s + 1) * 128],
                                   wkf[64 * par:64 * par + 64, :],
                                   start=True, stop=True)
                return ins
            s_kq = P.add("pe", kqmm, deps=kdeps, tag=f"A{t}.kq{h}")
            edeps = [s_kq]
            g2 = last_exp.get(("kp", par))
            if g2 is not None:
                edeps.append(g2)
            s_exp = P.add("act",
                          lambda e, par=par, bank=bank: e.activation(
                              kp_bufs[par][:, :], bank[:, :], AF.Exp),
                          deps=edeps, tag=f"A{t}.exp{h}")
            exps[h] = s_exp
            last_exp[("bank", par)] = s_exp

        emit_kq(0)
        emit_kq(1)
        for h in range(H):
            par = h % 2
            cdeps = [exps[h], vevs[3]]
            if last_ctx_ev[par] is not None:
                cdeps.append(last_ctx_ev[par])

            def ctxmm(e, h=h, par=par):
                ins = None
                cslot = PC[0:65, 0:256] if par == 0 else PS[0:65, 0:256]
                for s in range(4):
                    ins = e.matmul(cslot, vv[:, s, h, :],
                                   kp_bufs[par][:, s * 256:(s + 1) * 256],
                                   start=(s == 0), stop=(s == 3))
                return ins
            s_ctx = P.add("pe", ctxmm, deps=cdeps, tag=f"A{t}.ctx{h}")
            last_exp[("kp", par)] = s_ctx

            def ctxev(e, h=h, par=par, t=t):
                dst = ctxacc[(t + 1) % 2][:, h * FD:(h + 1) * FD]
                srcp = PC[0:65, 0:256] if par == 0 else PS[0:65, 0:256]
                if t == 0:
                    return e.tensor_copy(dst, srcp)
                return e.scalar_tensor_tensor(dst, srcp, 1.0,
                                              ctxacc[t % 2][:, h * FD:(h + 1) * FD],
                                              ALU.mult, ALU.add)
            ev_deps = [s_ctx]
            if h in ctx_ev:
                ev_deps.append(ctx_ev[h])
            s_cev = P.add("dve", ctxev, deps=ev_deps, tag=f"A{t}.ctxev{h}")
            ctx_ev[h] = s_cev
            last_ctx_ev[par] = s_cev
            if h + 2 < H:
                emit_kq(h + 2)

    st0 = emitA_p1a(0)
    affs_q = {0: emitA_p1c(0, *st0)}
    trs0 = emitA_p1b(0, affs_q[0])
    parts_c = emitA_p2(0, trs0)
    st_q = {1: emitA_p1a(1)}
    affs_q[1] = emitA_p1c(1, *st_q[1])
    for t in range(nt_a):
        if t + 1 < nt_a:
            trs_n = emitA_p1b(t + 1, affs_q[t + 1])
        if t + 2 < nt_a:
            st_q[t + 2] = emitA_p1a(t + 2)
        emitA_p3(t, *parts_c)
        if t + 2 < nt_a:
            affs_q[t + 2] = emitA_p1c(t + 2, *st_q[t + 2])
        if t + 1 < nt_a:
            parts_c = emitA_p2(t + 1, trs_n)

    ctxfin = []
    ctx_final_buf = ctxacc[nt_a % 2]
    for h in range(H):
        for fc in range(2):
            s_t = P.add(
                "pe",
                lambda e, h=h, fc=fc: e.transpose(
                    PS[0:128, 0:65],
                    ctx_final_buf[0:65, h * FD + fc * 128: h * FD + (fc + 1) * 128],
                    eyef[0:65, 0:65]),
                deps=[ctx_ev[h], ctx_ev[H - 1], ctx_ev[H - 2], s_eyef] +
                     ([ctxfin[-1]] if ctxfin else []),
                tag=f"ctf{h}.{fc}")
            s_c0 = P.add("dve", lambda e: e.tensor_copy(cfix[:, 0:65], PS[0:128, 0:65]),
                         deps=[s_t], tag=f"ctf{h}.{fc}.c0")
            s_c1 = P.add("dve",
                         lambda e, h=h: e.tensor_scalar(cfix[:, 65:130],
                                                        bvrep[:, h * 65:(h + 1) * 65],
                                                        cfix[:, 64:65], None, ALU.mult),
                         deps=[s_c0, s_bvr], tag=f"ctf{h}.{fc}.c1")
            s_c2 = P.add("dve",
                         lambda e, h=h, fc=fc: e.tensor_tensor(
                             ctxT[:, (h * 2 + fc) * 65:(h * 2 + fc + 1) * 65],
                             cfix[:, 0:65], cfix[:, 65:130], ALU.add),
                         deps=[s_c1], tag=f"ctf{h}.{fc}.c2")
            ctxfin.append(s_c2)

    if stage == 1:
        dumps = []
        if os.environ.get("ATT_DUMP_V"):
            s_cv = P.add("act", lambda e: e.activation(x_tile[0], v_A[:, 0:4096],
                                                       AF.Copy),
                         deps=[ctxfin[-1]], tag="dumpv.c")
            for i in range(4):
                dumps.append(dma("sp", out_d[4096 + 128 * i:4096 + 128 * (i + 1), :],
                                 x_tile[0][:, 1024 * i:1024 * (i + 1)],
                                 deps=[s_cv], tag=f"dumpv{i}"))
            s_ck = P.add("act", lambda e: e.activation(x_tile[1], kp_A[:, 0:1024]
                                                       if False else kT_A[:, 0:4096],
                                                       AF.Copy),
                         deps=[ctxfin[-1]], tag="dumpk.c")
            for i in range(4):
                dumps.append(dma("sp", out_d[5120 + 128 * i:5120 + 128 * (i + 1), :],
                                 x_tile[1][:, 1024 * i:1024 * (i + 1)],
                                 deps=[s_ck], tag=f"dumpk{i}"))
        for i in range(4):
            dumps.append(dma("sp", out_d[65 * i:65 * i + 65, :],
                             ctx_final_buf[0:65, i * 1024:(i + 1) * 1024],
                             deps=[ctxfin[-1]], tag=f"dump{i}"))
        P.add("sp", lambda e: None, deps=dumps, tag="final_wait")
        with nc.Block() as block:
            emctx = P.emit(nc, block)
        emctx.close()
        ctx.close()
        return nc

    s_wq = dma("pool", wq_sb.rearrange("p (dc c) -> p dc c", c=D),
               wq_d.rearrange("(dc p) c -> p dc c", p=128),
               deps=[last_kmm, prev_kev], tag="wq")
    s_wp = dma("pool", wp_sb.rearrange("p (dc c) -> p dc c", c=D),
               wp_d.rearrange("(dc p) c -> p dc c", p=128),
               deps=[last_vev], tag="wp")

    nt_b = SEQ // TG_B
    xbuf_guard = {0: list(xbuf_guard[0]), 1: list(xbuf_guard[0]),
                  2: list(xbuf_guard[1]), 3: list(xbuf_guard[1])}
    ysum_ev = None
    prev_ln_b = list(prev_ln)
    prev_trs_b = prev_trs
    prev_qev = prev_kev
    last_ps_guard = None
    kq_guard = {0: None, 1: None}
    last_x1 = {0: None, 1: None}
    last_yred = {0: None, 1: None}
    x1c_steps = {}
    last_x1st = None
    last_yst = None
    last_attT = {0: None, 1: None}
    last_mults = {0: None, 1: None}
    last_pc_guard = None
    last_ysmm = None
    x1_store_steps = []
    yt_store_steps = []
    last_qp = [None, None]
    B_last_proj = None
    B_last_qmm = None
    prev_ytev = None

    ctv = ctxT.rearrange("p (s f) -> p s f", f=65)

    def slot_ap(st, h):
        if h < 7:
            base = PV if st == 0 else PK
            return base[0:128, h * 65: h * 65 + 65]
        if h < 14:
            base = PV if st == 0 else PK
            return base[0:128, 512 + (h - 7) * 65: 512 + (h - 7) * 65 + 65]
        off = (0 if st == 0 else 256) + (h - 14) * 65
        return PC[0:128, off: off + 65]

    def emitB_p1a(t):
        buf = t % 4
        xa = x_tileB[buf]
        s_ldx = dma("sp", xa.rearrange("p (s d) -> p s d", d=D),
                    x_d[t * TG_B:(t + 1) * TG_B, :].rearrange("(s p) d -> p s d", p=128),
                    deps=xbuf_guard[buf], tag=f"B{t}.ldx")
        xbuf_guard[buf] = []
        ln_deps = [s_ldx] + prev_ln_b + ([prev_trs_b] if prev_trs_b is not None else [])
        s_v2, s_red = ln_stats(xa, 2, ln_deps, f"B{t}.ln")
        xbuf_guard[buf].append(s_red)
        return xa, s_v2

    def emitB_p1c(t, xa, s_v2):
        nonlocal prev_ln_b
        affs = ln_apply(xa, 2, xn_sb, f"B{t}.ln", s_v2, extra=[prev_trs_b])
        prev_ln_b = affs
        xbuf_guard[t % 4].append(affs[-1])
        return affs

    def emitB_p1b(t, affs, qev_guard):
        nonlocal prev_trs_b
        trs = tp_steps(xn_sb, xnT_B, 2, TG_B,
                       affs + [prev_ytev, last_attT[0], last_attT[1], kq_guard[1]],
                       f"B{t}", extra_guard=qev_guard)
        prev_trs_b = trs[-1]
        return trs

    def emitB_p2(t, trs):
        nonlocal prev_qev, B_last_qmm
        qevs = []
        for cp in range(4):
            bank = cp % 2
            pb = PX[:, bank * 512:(bank + 1) * 512]
            mdeps = [trs[-1], s_wq]
            if len(qevs) >= 2:
                mdeps.append(qevs[-2])
            else:
                if prev_qev is not None:
                    mdeps.append(prev_qev)
                mdeps += [last_attT[0], last_attT[1], prev_ytev, kq_guard[1]]

            def qmm(e, cp=cp, pb=pb):
                ins = None
                wqv = wq_sb.rearrange("p (dc c) -> p dc c", c=D)
                xv = xnT_B.rearrange("p (dc t) -> p dc t", t=TG_B)
                for ci in range(2):
                    c = 2 * cp + ci
                    for dc in range(8):
                        ins = e.matmul(pb[:, ci * 256:(ci + 1) * 256],
                                       wqv[:, dc, c * 128:(c + 1) * 128], xv[:, dc, :],
                                       start=(dc == 0), stop=False)
                    ins = e.matmul(pb[:, ci * 256:(ci + 1) * 256],
                                   bqrow[0:1, c * 128:(c + 1) * 128],
                                   ones_row[0:1, 0:TG_B], start=False, stop=True)
                return ins
            s_qmm = P.add("pe", qmm, deps=mdeps + [s_bqr, s_ones], tag=f"B{t}.qmm{cp}")
            B_last_qmm = s_qmm
            qevs.append(P.add(
                "act",
                lambda e, cp=cp, pb=pb: e.activation(
                    qT_B[:, 2 * cp * TG_B:(2 * cp + 2) * TG_B], pb,
                    AF.Identity, bias=0.0, scale=1.0),
                deps=[s_qmm], tag=f"B{t}.qev{cp}"))
        prev_qev = qevs[-1]
        return [qevs[c // 2] for c in range(8)]

    def emitB_heads(t, xa, qevs):
        nonlocal last_ps_guard, last_pc_guard
        buf = t % 4
        qTv = qT_B.rearrange("p (c t) -> p c t", t=TG_B)
        einsum_last = {0: [], 1: []}
        exps = {}

        def emit_kq(h):
            par = h % 2
            kbank = PS[:, 0:512] if par == 0 else PX[:, 0:512]
            kdeps = [qevs[h // 2], s_wqf]
            g = kq_guard[par]
            if g is not None:
                kdeps.append(g)
            elif t == 0 and h == 0:
                kdeps.append(ctxfin[-1])
            if h <= 1:
                kdeps += [prev_ytev, qevs[-1]]

            def kqmm(e, h=h, kbank=kbank):
                ins = None
                for fh in range(2):
                    ins = e.matmul(kbank[:, fh * 256:(fh + 1) * 256],
                                   wqf[64 * (h % 2):64 * (h % 2) + 64,
                                       fh * 128:(fh + 1) * 128],
                                   qTv[64 * (h % 2):64 * (h % 2) + 64, h // 2, :],
                                   start=True, stop=True)
                return ins
            s_kq = P.add("pe", kqmm, deps=kdeps, tag=f"B{t}.kq{h}")
            edeps = [s_kq]
            if last_qp[par] is not None:
                edeps.append(last_qp[par])
            s_exp = P.add("act",
                          lambda e, par=par, kbank=kbank: e.activation(
                              qpT_B[par][:, :], kbank, AF.Exp),
                          deps=edeps, tag=f"B{t}.qexp{h}")
            kq_guard[par] = s_exp
            exps[h] = s_exp

        emit_kq(0)
        emit_kq(1)
        for h in range(H):
            par = h % 2
            s_exp = exps[h]
            for st in range(2):
                mdeps = [s_exp]
                if t == 0 and h == 0:
                    mdeps.append(ctxfin[-1])
                if h == 0:
                    mdeps += [last_mults[st], last_x1[st]]
                if h == 14 and last_pc_guard is not None:
                    mdeps.append(last_pc_guard)

                def emm(e, h=h, par=par, st=st):
                    ins = None
                    for fc in range(2):
                        ins = e.matmul(slot_ap(st, h),
                                       qpT_B[par][:, fc * 256 + st * 128:
                                                  fc * 256 + st * 128 + 128],
                                       ctv[:, h * 2 + fc, :],
                                       start=(fc == 0), stop=(fc == 1))
                    return ins
                s_e = P.add("pe", emm, deps=mdeps, tag=f"B{t}.ein{h}.{st}")
                einsum_last[st].append(s_e)
            last_qp[par] = einsum_last[1][-1]
            if h + 2 < H:
                emit_kq(h + 2)
        last_ps_guard = exps[H - 1]

        return einsum_last

    def emitB_tail(t, xa, qevs, einsum_last, qevs_next, trs_next):
        nonlocal last_x1st, B_last_proj
        proj_steps = []
        for st in range(2):
            base = PV if st == 0 else PK
            ein_done = [einsum_last[st][-1], einsum_last[1][-1]]

            def dext(e, st=st, base=base):
                e.tensor_scalar(dsc[:, st * 32 + 0:st * 32 + 7],
                                base[0:128, 0:455].rearrange(
                                    "p (s f) -> p s f", f=65)[:, :, 64],
                                1e-6, None, ALU.add)
                e.tensor_scalar(dsc[:, st * 32 + 7:st * 32 + 14],
                                base[0:128, 512:967].rearrange(
                                    "p (s f) -> p s f", f=65)[:, :, 64],
                                1e-6, None, ALU.add)
                off = 0 if st == 0 else 256
                return e.tensor_scalar(dsc[:, st * 32 + 14:st * 32 + 16],
                                       PC[0:128, off:off + 130].rearrange(
                                           "p (s f) -> p s f", f=65)[:, :, 64],
                                       1e-6, None, ALU.add)
            s_dx = P.add("dve", dext, deps=ein_done, tag=f"B{t}.dext{st}")
            s_dr = P.add("dve",
                         lambda e, st=st: e.reciprocal(dsc[:, st * 32 + 16:st * 32 + 32],
                                                       dsc[:, st * 32 + 0:st * 32 + 16]),
                         deps=[s_dx], tag=f"B{t}.drec{st}")
            mdeps0 = [s_dr, last_attT[st]]

            def dmulA(e, st=st, base=base):
                return e.tensor_tensor(
                    atok_B[st][:, 0:448].rearrange("p (s k) -> p s k", k=64),
                    base[0:128, 0:455].rearrange("p (s f) -> p s f", f=65)[:, :, 0:64],
                    dsc[:, st * 32 + 16:st * 32 + 23].to_broadcast([128, 7, 64]),
                    ALU.mult)

            def dmulB(e, st=st, base=base):
                return e.tensor_tensor(
                    atok_B[st][:, 448:896].rearrange("p (s k) -> p s k", k=64),
                    base[0:128, 512:967].rearrange("p (s f) -> p s f", f=65)[:, :, 0:64],
                    dsc[:, st * 32 + 23:st * 32 + 30].to_broadcast([128, 7, 64]),
                    ALU.mult)

            def dmulC(e, st=st):
                off = 0 if st == 0 else 256
                return e.tensor_tensor(
                    atok_B[st][:, 896:1024].rearrange("p (s k) -> p s k", k=64),
                    PC[0:128, off:off + 130].rearrange("p (s f) -> p s f", f=65)[:, :, 0:64],
                    dsc[:, st * 32 + 30:st * 32 + 32].to_broadcast([128, 2, 64]),
                    ALU.mult)
            mults = [P.add("dve", dmulA, deps=mdeps0, tag=f"B{t}.dmulA{st}"),
                     P.add("dve", dmulB, deps=[], tag=f"B{t}.dmulB{st}"),
                     P.add("dve", dmulC, deps=[], tag=f"B{t}.dmulC{st}")]
            last_mults[st] = mults[-1]

            pb = PXb[:, st * 1024:(st + 1) * 1024]
            tdeps = [mults[-1], qevs[-1], last_attT[st], prev_trs_b, prev_ytev]
            if qevs_next is not None:
                tdeps.append(qevs_next[-1])
            if trs_next is not None:
                tdeps.append(trs_next[-1])
            tps = []
            for c in range(8):
                tps.append(P.add(
                    "pe",
                    lambda e, c=c, st=st, pb=pb: e.transpose(
                        pb[:, c * 128:(c + 1) * 128],
                        atok_B[st][:, c * 128:(c + 1) * 128], eye[:]),
                    deps=tdeps if c == 0 else [], tag=f"B{t}.atp{st}.{c}"))
            s_aev = P.add("dve",
                          lambda e, st=st, pb=pb: e.tensor_copy(attT_B[st][:, :], pb),
                          deps=[tps[-1]], tag=f"B{t}.atpev{st}")
            last_attT[st] = s_aev

            pdeps = [s_aev, s_wp, s_bpr, s_ones, mults[-1]]

            def pmm(e, st=st, base=base):
                ins = None
                wpv = wp_sb.rearrange("p (dc c) -> p dc c", c=D)
                av = attT_B[st].rearrange("p (dc t) -> p dc t", t=128)
                for half in range(2):
                    pb2 = base[:, half * 512:(half + 1) * 512]
                    for dc in range(8):
                        ins = e.matmul(pb2, av[:, dc, :],
                                       wpv[:, dc, half * 512:(half + 1) * 512],
                                       start=(dc == 0), stop=False)
                    ins = e.matmul(pb2, ones_row[0:1, 0:128],
                                   bprow[0:1, half * 512:(half + 1) * 512],
                                   start=False, stop=True)
                return ins
            s_proj = P.add("pe", pmm, deps=pdeps, tag=f"B{t}.proj{st}")
            proj_steps.append(s_proj)
            B_last_proj = s_proj

            xdeps = [s_proj, last_x1[st], last_x1st, last_yred[t % 2]]
            x1sb = x1sb_bufs[t % 2]
            buf = t % 4

            def x1c(e, st=st, base=base, xa=xa, x1sb=x1sb):
                e.scalar_tensor_tensor(x1sb[:, st * D:st * D + 512], base[:, 0:512], 1.0,
                                       xa[:, st * D:st * D + 512], ALU.mult, ALU.add)
                return e.scalar_tensor_tensor(x1sb[:, st * D + 512:(st + 1) * D],
                                              base[:, 512:1024], 1.0,
                                              xa[:, st * D + 512:(st + 1) * D],
                                              ALU.mult, ALU.add)
            s_x1 = P.add("dve", x1c, deps=xdeps, tag=f"B{t}.x1{st}")
            last_x1[st] = s_x1
            x1c_steps.setdefault(t, []).append(s_x1)
            xbuf_guard[t % 4].append(s_x1)

        s_x1st = dma("act", x1_scr[t * TG_B:(t + 1) * TG_B, :]
                     .rearrange("(s p) d -> p s d", p=128),
                     x1sb.rearrange("p (s d) -> p s d", d=D),
                     deps=[last_x1[0], last_x1[1]], tag=f"B{t}.x1st")
        last_x1st = s_x1st
        x1_store_steps.append(s_x1st)
        return proj_steps

    def emitB_p4bS(t):
        nonlocal last_ysmm
        x1sb = x1sb_bufs[t % 2]
        ysb = ysb_bufs[t % 2]
        yln_deps = list(x1c_steps[t])
        if last_ysmm is not None:
            yln_deps.append(last_ysmm)
        s_v2y, s_yred = ln_stats(x1sb, 2, yln_deps, f"B{t}.yln", si=1)
        yaffs = ln_apply(x1sb, 2, ysb, f"B{t}.yln", s_v2y, si=1)
        last_yred[t % 2] = yaffs[-1]
        return yaffs

    def emitB_p4b(t, yaffs, proj_steps, qevs_next):
        nonlocal last_ps_guard, last_pc_guard, last_ysmm, ysum_ev
        nonlocal last_yst, prev_ytev
        ysb = ysb_bufs[t % 2]
        ytdeps = list(yaffs) + [prev_trs_b]
        if last_yst is not None:
            ytdeps.append(last_yst)
        if qevs_next is not None:
            ytdeps.append(qevs_next[-1])
        ytevs = tp_steps(ysb, ytout_B[t % 2], 2, TG_B, ytdeps, f"B{t}.yt",
                         extra_guard=proj_steps[-1])
        prev_ytev = ytevs[-1]
        s_ytst = dma("act",
                     yt_scr[:, t * TG_B:(t + 1) * TG_B]
                     .rearrange("(dc p) t -> p dc t", p=128),
                     ytout_B[t % 2].rearrange("p (dc t) -> p dc t", t=TG_B),
                     deps=ytevs, tag=f"B{t}.ytst")
        last_yst = s_ytst
        yt_store_steps.append(s_ytst)

        ydeps = yaffs + [s_onesc, kq_guard[0], kq_guard[1]]

        def ysmmA(e):
            ins = None
            for st in range(2):
                ins = e.matmul(PS[0:1, 0:512], ones_col[:, :],
                               ysb[:, st * D: st * D + 512],
                               start=(st == 0), stop=(st == 1))
            return ins

        def ysmmB(e):
            ins = None
            for st in range(2):
                ins = e.matmul(PC[0:1, 0:512], ones_col[:, :],
                               ysb[:, st * D + 512: st * D + 1024],
                               start=(st == 0), stop=(st == 1))
            return ins
        s_ysA = P.add("pe", ysmmA, deps=ydeps, tag=f"B{t}.ysA")
        s_ysB = P.add("pe", ysmmB, deps=ydeps + [last_mults[0], last_mults[1]],
                      tag=f"B{t}.ysB")
        last_ysmm = s_ysB

        def ysev(e, t=t):
            src = rows[0:1, (t % 2) * 1024:(t % 2) * 1024 + 1024]
            dst = rows[0:1, ((t + 1) % 2) * 1024:((t + 1) % 2) * 1024 + 1024]
            if t == 0:
                e.tensor_copy(dst[0:1, 0:512], PS[0:1, 0:512])
                return e.tensor_copy(dst[0:1, 512:1024], PC[0:1, 0:512])
            e.scalar_tensor_tensor(dst[0:1, 0:512], PS[0:1, 0:512], 1.0,
                                   src[0:1, 0:512], ALU.mult, ALU.add)
            return e.scalar_tensor_tensor(dst[0:1, 512:1024], PC[0:1, 0:512], 1.0,
                                          src[0:1, 512:1024], ALU.mult, ALU.add)
        ysum_ev = P.add("dve", ysev, deps=[s_ysA, s_ysB] +
                        ([ysum_ev] if ysum_ev is not None else []), tag=f"B{t}.ysev")
        last_ps_guard = ysum_ev
        kq_guard[0] = ysum_ev
        last_pc_guard = ysum_ev

    stb = {0: emitB_p1a(0)}
    affb = {0: emitB_p1c(0, *stb[0])}
    trsb = emitB_p1b(0, affb[0], prev_qev)
    qevs_c = emitB_p2(0, trsb)
    stb[1] = emitB_p1a(1)
    affb[1] = emitB_p1c(1, *stb[1])
    xa_c = stb[0][0]
    pst_q = {}
    yaffs_q = {}
    for t in range(nt_b):
        if t >= 1:
            yaffs_q[t - 1] = emitB_p4bS(t - 1)
        if t + 2 < nt_b:
            stb[t + 2] = emitB_p1a(t + 2)
        eins = emitB_heads(t, xa_c, qevs_c)
        if t + 1 < nt_b:
            trs_nb = emitB_p1b(t + 1, affb[t + 1], qevs_c[-1])
            qevs_n = emitB_p2(t + 1, trs_nb)
        else:
            trs_nb = None
            qevs_n = None
        pst_q[t] = emitB_tail(t, xa_c, qevs_c, eins, qevs_n, trs_nb)
        if t + 2 < nt_b:
            affb[t + 2] = emitB_p1c(t + 2, *stb[t + 2])
        if t >= 1:
            emitB_p4b(t - 1, yaffs_q[t - 1], pst_q[t - 1], qevs_n)
        if t + 1 < nt_b:
            qevs_c = qevs_n
            xa_c = stb[t + 1][0]
    yaffs_q[nt_b - 1] = emitB_p4bS(nt_b - 1)
    emitB_p4b(nt_b - 1, yaffs_q[nt_b - 1], pst_q[nt_b - 1], None)

    ysum_final = rows[0:1, (nt_b % 2) * 1024:(nt_b % 2) * 1024 + 1024]

    if stage == 2:
        dumps = []
        for i in range(8):
            dumps.append(dma("sp", out_d[i * 1024:(i + 1) * 1024, :],
                             x1_scr[i * 1024:(i + 1) * 1024, :],
                             deps=[x1_store_steps[-1], ysum_ev], tag=f"dump{i}"))
        P.add("sp", lambda e: None, deps=dumps, tag="final_wait")
        with nc.Block() as block:
            emctx = P.emit(nc, block)
        emctx.close()
        ctx.close()
        return nc

    s_wr8 = dma("sp", cfix[0:8, 0:128].bitcast(BF16)[0:8, 0:256] if False else wr8_sb,
                wrt_d[:, :], tag="R.wr8")
    s_yb = P.add("pe", lambda e: (
        e.matmul(PV[0:8, 0:512], ones8f[0:1, :], ysum_final[0:1, 0:512],
                 start=True, stop=True),
        e.matmul(PK[0:8, 0:512], ones8f[0:1, :], ysum_final[0:1, 512:1024],
                 start=True, stop=True))[-1],
        deps=[ysum_ev, s_ones8, last_mults[0], last_mults[1], last_x1[0], last_x1[1]],
        tag="R.yb")
    s_prod = P.add("dve", lambda e: (
        e.tensor_tensor(prod8[0:8, 0:512], PV[0:8, 0:512], wr8_sb[0:8, 0:512],
                        ALU.mult),
        e.tensor_tensor(prod8[0:8, 512:1024], PK[0:8, 0:512], wr8_sb[0:8, 512:1024],
                        ALU.mult))[-1],
        deps=[s_yb, s_wr8], tag="R.prod")
    s_rsum = P.add("dve", lambda e: e.reduce_sum(prod8[0:8, 1024:1025],
                                                 prod8[0:8, 0:1024], axis=AXX),
                   deps=[s_prod], tag="R.rsum")
    s_sct = P.add("pe", lambda e: e.transpose(PS[0:8, 0:8], prod8[0:8, 1024:1032],
                                              eyef[0:8, 0:8]),
                  deps=[s_rsum, s_eyef, last_ps_guard, s_pad8], tag="R.sct")
    score_steps = [P.add("dve", lambda e: e.tensor_copy(dsc[0:1, 32:40],
                                                        PS[0:1, 0:8]),
                         deps=[s_sct], tag="R.scv")]
    ysum_pad = P.add("dve", lambda e: e.memset(prod8[0:8, 1025:1032], 0.0),
                     deps=[s_rsum], tag="R.pad") if False else None
    s_br = dma("sp", dsc[0:1, 40:48], br_d[:, :], deps=[ysum_ev], tag="R.br")
    s_sc = P.add("dve",
                 lambda e: e.tensor_scalar(dsc[0:1, 48:56], dsc[0:1, 32:40],
                                           1.0 / SEQ, None, ALU.mult),
                 deps=score_steps + [s_br], tag="R.scale")
    s_sc2 = P.add("dve",
                  lambda e: e.tensor_tensor(dsc[0:1, 0:8], dsc[0:1, 48:56],
                                            dsc[0:1, 40:48], ALU.add),
                  deps=[s_sc], tag="R.addbr")
    s_mx = P.add("dve", lambda e: e.reduce_max(dsc[0:1, 8:9], dsc[0:1, 0:8], axis=AXX),
                 deps=[s_sc2], tag="R.max")
    s_msk = P.add("dve",
                  lambda e: e.tensor_scalar(dsc[0:1, 16:24], dsc[0:1, 0:8],
                                            dsc[0:1, 8:9], None, ALU.is_ge),
                  deps=[s_mx], tag="R.mask")
    s_t0 = P.add("dve",
                 lambda e: e.tensor_scalar(dsc[0:1, 24:32], dsc[0:1, 16:24],
                                           -8.0, None, ALU.mult),
                 deps=[s_msk], tag="R.t0")
    s_t1 = P.add("dve",
                 lambda e: e.tensor_tensor(dsc[0:1, 16:24], dsc[0:1, 24:32],
                                           rowc[0:1, 0:8], ALU.add),
                 deps=[s_t0, s_rowc], tag="R.t1")
    s_emin = P.add("dve",
                   lambda e: e.tensor_reduce(dsc[0:1, 9:10], dsc[0:1, 16:24],
                                             AXX, ALU.min),
                   deps=[s_t1], tag="R.emin")
    s_ef = P.add("dve",
                 lambda e: e.tensor_scalar(dsc[0:1, 10:11], dsc[0:1, 9:10],
                                           0.0, None, ALU.add),
                 deps=[s_emin], tag="R.ef")
    s_ebf = P.add("act", lambda e: e.activation(ebf[0:1, :], dsc[0:1, 10:11], AF.Copy),
                  deps=[s_ef], tag="R.ebf")
    s_erepmm = P.add("pe",
                     lambda e: e.matmul(PS[0:128, 0:1], ones_row[0:1, 0:128], ebf[0:1, :],
                                        start=True, stop=True),
                     deps=[s_ebf, s_ones, ysum_ev], tag="R.erep")
    s_erep = P.add("dve", lambda e: e.tensor_copy(erep[:], PS[0:128, 0:1]),
                   deps=[s_erepmm], tag="R.erepc")
    s_i1a = P.add("dve",
                  lambda e: e.tensor_scalar(idxf[:, 0:1], erep[:], 1024.0, None, ALU.mult),
                  deps=[s_erep], tag="R.i1a")
    s_i1b = P.add("dve",
                  lambda e: e.tensor_tensor(idxf[:, 1:2], idxf[:, 0:1], iotap[:], ALU.add),
                  deps=[s_i1a, s_iota], tag="R.i1b")
    s_i2a = P.add("dve",
                  lambda e: e.tensor_scalar(idxf[:, 2:3], erep[:], 4096.0, None, ALU.mult),
                  deps=[s_erep], tag="R.i2a")
    s_i2b = P.add("dve",
                  lambda e: e.tensor_tensor(idxf[:, 3:4], idxf[:, 2:3], iotap[:], ALU.add),
                  deps=[s_i2a, s_iota], tag="R.i2b")
    s_ib1a = P.add("dve",
                   lambda e: e.tensor_scalar(dsc[0:32, 32:33], erep[0:32, :], 32.0,
                                             None, ALU.mult),
                   deps=[s_erep], tag="R.ib1a")
    s_ib1b = P.add("dve",
                   lambda e: e.tensor_tensor(dsc[0:32, 33:34], dsc[0:32, 32:33],
                                             iotap[0:32, :], ALU.add),
                   deps=[s_ib1a], tag="R.ib1b")
    s_ci1 = P.add("pool", lambda e: e.tensor_copy(idx1[:], idxf[:, 1:2]),
                  deps=[s_i1b], tag="R.ci1")
    s_ci2 = P.add("pool", lambda e: e.tensor_copy(idx2[:], idxf[:, 3:4]),
                  deps=[s_i2b], tag="R.ci2")
    s_cib1 = P.add("pool", lambda e: e.tensor_copy(idxb1[:], dsc[0:32, 33:34]),
                   deps=[s_ib1b], tag="R.cib1")
    s_cib2 = P.add("pool", lambda e: e.tensor_copy(idxb2[:], erep[0:2, :]),
                   deps=[s_erep], tag="R.cib2")

    s_gb1 = P.add("pool",
                  lambda e: e.indirect_dma_start(
                      out=cfix[0:32, 0:128], out_offset=None,
                      in_=b1_d[:, :], in_offset=IndirectOffsetOnAxis(ap=idxb1[:], axis=0)),
                  deps=[s_cib1, ctxfin[-1]], dma=True, tag="G.b1")
    s_gb2 = P.add("pool",
                  lambda e: e.indirect_dma_start(
                      out=rowbuf[0:2, :], out_offset=None,
                      in_=b2_d[:, :], in_offset=IndirectOffsetOnAxis(ap=idxb2[:], axis=0)),
                  deps=[s_cib2, score_steps[-1]], dma=True, tag="G.b2")
    s_b1t = P.add("pe",
                  lambda e: e.transpose(PS[0:128, 0:32], cfix[0:32, 0:128],
                                        eyef[0:32, 0:32]),
                  deps=[s_gb1, s_erep, s_eyef], tag="G.b1t")
    s_b1c = P.add("dve", lambda e: e.tensor_copy(b1selT[:, :], PS[0:128, 0:32]),
                  deps=[s_b1t], tag="G.b1c")
    s_b2c = P.add("act", lambda e: e.activation(b2row[0:1, :], rowbuf[0:1, :], AF.Copy),
                  deps=[s_gb2], tag="G.b2c")

    if stage == 26:
        dumps = []
        s_dsc = P.add("act", lambda e: e.activation(x_tile[0][:, 0:64], dsc[:, :],
                                                    AF.Copy),
                      deps=[s_erep, s_i1b, s_i2b, s_ib1b, s_cib1, s_cib2, s_ci1, s_ci2],
                      tag="dump.dsc")
        dumps.append(dma("sp", out_d[0:128, 0:64], x_tile[0][:, 0:64],
                         deps=[s_dsc], tag="dump.d0", slow=True))
        s_dix = P.add("act", lambda e: e.activation(x_tile[0][:, 64:68], idxf[:, :],
                                                    AF.Copy),
                      deps=[s_dsc], tag="dump.idx")
        dumps.append(dma("sp", out_d[0:128, 64:68], x_tile[0][:, 64:68],
                         deps=[s_dix], tag="dump.d1", slow=True))
        s_der = P.add("act", lambda e: e.activation(x_tile[0][:, 68:69], erep[:, :],
                                                    AF.Copy),
                      deps=[s_dsc], tag="dump.er")
        dumps.append(dma("sp", out_d[0:128, 68:69], x_tile[0][:, 68:69],
                         deps=[s_der], tag="dump.d2", slow=True))
        P.add("sp", lambda e: None, deps=dumps, tag="final_wait")
        with nc.Block() as block:
            emctx = P.emit(nc, block)
        emctx.close()
        ctx.close()
        return nc

    g_w1 = []
    for dc in range(8):
        g_w1.append(P.add(
            "pool",
            lambda e, dc=dc: e.indirect_dma_start(
                out=w1sel[:, dc * DFF:(dc + 1) * DFF], out_offset=None,
                in_=w1_d[:, :], in_offset=IndirectOffsetOnAxis(ap=idx1[:], axis=0),
                element_offset=dc * 128 * DFF),
            deps=[s_ci1, score_steps[-1], s_b2c, s_b1c] + yt_store_steps[-8:],
            dma=True, tag=f"G.w1.{dc}"))
    g_w2 = []
    for fc in range(32):
        g_w2.append(P.add(
            "pool",
            lambda e, fc=fc: e.indirect_dma_start(
                out=w2sel[:, fc * D:(fc + 1) * D], out_offset=None,
                in_=w2_d[:, :], in_offset=IndirectOffsetOnAxis(ap=idx2[:], axis=0),
                element_offset=fc * 128 * D),
            deps=[s_ci2, score_steps[-1]] + x1_store_steps[-8:] + yt_store_steps[-8:],
            dma=True, tag=f"G.w2.{fc}"))
    if stage == 25:
        dumps = []
        s_dsc = P.add("act", lambda e: e.activation(x_tile[0][:, 0:64], dsc[:, :],
                                                    AF.Copy),
                      deps=[s_b1c, s_b2c, s_erep, s_i1b, s_i2b], tag="dump.dsc")
        dumps.append(dma("sp", out_d[0:128, 0:64], x_tile[0][:, 0:64],
                         deps=[s_dsc], tag="dump.d0", slow=True))
        s_dix = P.add("act", lambda e: e.activation(x_tile[0][:, 64:68], idxf[:, :],
                                                    AF.Copy),
                      deps=[s_dsc], tag="dump.idx")
        dumps.append(dma("sp", out_d[0:128, 64:68], x_tile[0][:, 64:68],
                         deps=[s_dix], tag="dump.d1", slow=True))
        s_w1c = P.add("act", lambda e: e.activation(x_tile[1][:, 0:1024],
                                                    w1sel[:, 0:1024], AF.Copy),
                      deps=[g_w1[-1]], tag="dump.w1c")
        dumps.append(dma("sp", out_d[128:256, :], x_tile[1][:, 0:1024],
                         deps=[s_w1c], tag="dump.d2"))
        s_w2c = P.add("act", lambda e: e.activation(x_tile[1][:, 1024:2048],
                                                    w2sel[:, 0:1024], AF.Copy),
                      deps=[g_w2[-1]], tag="dump.w2c")
        dumps.append(dma("sp", out_d[256:384, :], x_tile[1][:, 1024:2048],
                         deps=[s_w2c], tag="dump.d3"))
        s_b1d = P.add("act", lambda e: e.activation(x_tile[0][:, 68:100],
                                                    b1selT[:, :], AF.Copy),
                      deps=[s_b1c], tag="dump.b1")
        dumps.append(dma("sp", out_d[0:128, 68:100], x_tile[0][:, 68:100],
                         deps=[s_b1d], tag="dump.d4"))
        P.add("sp", lambda e: None, deps=dumps, tag="final_wait")
        with nc.Block() as block:
            emctx = P.emit(nc, block)
        emctx.close()
        ctx.close()
        return nc

    nt_c = SEQ // TG_C
    if stage == 30:
        nt_c = 1
    if stage == 31:
        nt_c = 1
    ybuf_guard = {0: [], 1: []}
    x1buf_guard = {0: [], 1: []}
    last_out = {0: None, 1: None}
    out_stores = []
    prev_hev_last = None
    for t in range(nt_c):
        buf = t % 2
        ydeps = [yt_store_steps[t]] + ybuf_guard[buf]
        if t < 2:
            ydeps += [B_last_proj, s_b1c]
        s_ldy = dma("sp", yT_C[buf].rearrange("p (dc t) -> p dc t", t=TG_C),
                    yt_scr[:, t * TG_C:(t + 1) * TG_C]
                    .rearrange("(dc p) t -> p dc t", p=128),
                    deps=ydeps, tag=f"C{t}.ldy")
        ybuf_guard[buf] = []
        xdeps = [x1_store_steps[t]] + x1buf_guard[buf]
        if t < 2:
            xdeps += [B_last_proj, B_last_qmm]
        s_ldx1 = dma("sp", x1_tile[buf].rearrange("p (s d) -> p s d", d=D),
                     x1_scr[t * TG_C:(t + 1) * TG_C, :]
                     .rearrange("(s p) d -> p s d", p=128),
                     deps=xdeps, tag=f"C{t}.ldx1")
        x1buf_guard[buf] = []

        skip_ffn2 = (stage in (31, 33))
        if stage == 32:
            dumps = []
            s_yc = P.add("dve", lambda e: e.tensor_copy(cfix[:, 0:65], eyef[:, 0:65]),
                         deps=[s_ldy, s_ldx1], tag="dump.nop")
            dumps.append(dma("sp", out_d[2048:2176, 0:65], cfix[:, 0:65],
                             deps=[s_yc], tag="dump.y0", slow=True))
            P.add("sp", lambda e: None, deps=dumps, tag="final_wait")
            with nc.Block() as block:
                emctx = P.emit(nc, block)
            emctx.close()
            ctx.close()
            return nc
        hevs = []
        w1v = w1sel.rearrange("p (dc f) -> p dc f", f=DFF)
        yv = yT_C[buf].rearrange("p (dc t) -> p dc t", t=TG_C)
        nfc = 1 if stage == 33 else 32
        for fc in range(nfc):
            pb = PX[:, (fc % 2) * 512:(fc % 2) * 512 + 256]
            mdeps = [s_ldy]
            if fc == 0:
                mdeps += g_w1
                if prev_hev_last is not None:
                    mdeps.append(prev_hev_last)
            if len(hevs) >= 2:
                mdeps.append(hevs[-2])

            def fmm(e, fc=fc, pb=pb, yv=yv):
                ins = None
                for dc in range(8):
                    ins = e.matmul(pb, w1v[:, dc, fc * 128:(fc + 1) * 128], yv[:, dc, :],
                                   start=(dc == 0), stop=(dc == 7))
                return ins
            s_f = P.add("pe", fmm, deps=mdeps, tag=f"C{t}.f1.{fc}")
            hevs.append(P.add(
                "act",
                lambda e, fc=fc, pb=pb: e.activation(h_C[:, fc * TG_C:(fc + 1) * TG_C],
                                                     pb, AF.Relu,
                                                     bias=b1selT[:, fc:fc + 1], scale=1.0),
                deps=[s_f, s_b1c], tag=f"C{t}.hev{fc}"))
        prev_hev_last = hevs[-1]

        if skip_ffn2 or stage == 33:
            break
        hv = h_C.rearrange("p (fc t) -> p fc t", t=TG_C)
        w2v = w2sel.rearrange("p (fc c) -> p fc c", c=D)
        for st in range(2):
            base = PV if st == 0 else PK
            mdeps = [hevs[-1], s_b2c, last_out[st]]
            if t == 0:
                mdeps += g_w2

            def f2mm(e, st=st, base=base):
                ins = None
                for half in range(2):
                    pb2 = base[:, half * 512:(half + 1) * 512]
                    for fc in range(32):
                        ins = e.matmul(pb2, hv[:, fc, st * 128:(st + 1) * 128],
                                       w2v[:, fc, half * 512:(half + 1) * 512],
                                       start=(fc == 0), stop=False)
                    ins = e.matmul(pb2, ones_row[0:1, 0:128],
                                   b2row[0:1, half * 512:(half + 1) * 512],
                                   start=False, stop=True)
                return ins
            s_f2 = P.add("pe", f2mm, deps=mdeps, tag=f"C{t}.f2.{st}")

            def outc(e, st=st, base=base, buf=buf):
                e.scalar_tensor_tensor(out_C[:, st * D:st * D + 512], base[:, 0:512],
                                       1.0, x1_tile[buf][:, st * D:st * D + 512],
                                       ALU.mult, ALU.add)
                return e.scalar_tensor_tensor(out_C[:, st * D + 512:(st + 1) * D],
                                              base[:, 512:1024], 1.0,
                                              x1_tile[buf][:, st * D + 512:(st + 1) * D],
                                              ALU.mult, ALU.add)
            s_oc = P.add("dve", outc, deps=[s_f2, s_ldx1], tag=f"C{t}.out{st}")
            last_out[st] = s_oc
            x1buf_guard[buf].append(s_oc)
            ybuf_guard[buf].append(s_oc)
        s_ost = dma("act", out_d[t * TG_C:(t + 1) * TG_C, :]
                    .rearrange("(s p) d -> p s d", p=128),
                    out_C.rearrange("p (s d) -> p s d", d=D),
                    deps=[last_out[0], last_out[1]], tag=f"C{t}.ost")
        out_stores.append(s_ost)
        last_out[0] = s_ost
        last_out[1] = s_ost

    if stage in (30, 31, 33):
        dumps = []
        nch = 4 if stage == 30 else 4
        s_hc = P.add("dve", lambda e: e.tensor_copy(x_tile[0][:, 0:4096].bitcast(BF16)[:, 0:4096][: , 0:4096] if False else x_tile[0][:, 0:2048], h_C[:, 0:2048]),
                     deps=[prev_hev_last], tag="dump.h")
        dumps.append(dma("sp", out_d[1024:1152, :], x_tile[0][:, 0:1024],
                         deps=[s_hc], tag="dump.h0"))
        dumps.append(dma("sp", out_d[1152:1280, :], x_tile[0][:, 1024:2048],
                         deps=[s_hc], tag="dump.h1"))
        if stage == 30:
            dumps.append(out_stores[-1])
            dumps.append(dma("sp", out_d[1280:1408, :],
                             out_C[:, 0:1024], deps=[out_stores[-1]], tag="dump.oc"))
        P.add("sp", lambda e: None, deps=dumps, tag="final_wait")
    else:
        P.add("sp", lambda e: None, deps=out_stores[-8:], tag="final_wait")

    with nc.Block() as block:
        emctx = P.emit(nc, block)
    emctx.close()
    ctx.close()
    return nc


_NC_CACHE = {}


def _get_nc():
    stage = int(os.environ.get("ATT_STAGE", "3"))
    if "nc" not in _NC_CACHE:
        _NC_CACHE["nc"] = build_program(stage)
    return _NC_CACHE["nc"]


def _stage_inputs(L0_attn_x, L0_attn_Wq, L0_attn_bq, L0_attn_Wk, L0_attn_bk, L0_attn_Wv,
                  L0_attn_bv, L0_attn_Wproj, L0_attn_bproj, L0_attn_wq_feat,
                  L0_attn_wk_feat, L0_ffn_Wr, L0_ffn_br, L0_ffn_W1, L0_ffn_b1,
                  L0_ffn_W2, L0_ffn_b2):
    f32 = np.float32

    def c(a):
        return np.ascontiguousarray(np.asarray(a, dtype=f32))

    x = c(L0_attn_x)
    bv_full = c(L0_attn_bv)
    bvrep = np.zeros((128, H * 65), dtype=f32)
    for h in range(H):
        bvrep[:, h * 65:h * 65 + 64] = bv_full[h * 64:(h + 1) * 64][None, :]
    wqf = np.ascontiguousarray(np.concatenate([c(L0_attn_wq_feat)] * 2, axis=0))
    wkf = np.ascontiguousarray(np.concatenate([c(L0_attn_wk_feat)] * 2, axis=0))
    eyebf = np.eye(128, dtype=np.float32).astype(ml_dtypes.bfloat16)
    eyef = np.eye(128, dtype=f32)
    iotap = np.arange(128, dtype=f32).reshape(128, 1).copy()
    rowc = np.zeros((1, 16), dtype=f32)
    rowc[0, 0:8] = np.arange(8, dtype=f32) + 8.0
    common = {
        "wq": c(L0_attn_Wq), "wk": c(L0_attn_Wk), "wv": c(L0_attn_Wv),
        "wp": c(L0_attn_Wproj), "bq": c(L0_attn_bq), "bk": c(L0_attn_bk),
        "bvrep": bvrep, "bproj": c(L0_attn_bproj).reshape(1, D),
        "wqf": wqf, "wkf": wkf,
        "wrt": np.ascontiguousarray(c(L0_ffn_Wr).T),
        "br": c(L0_ffn_br).reshape(1, NEXP),
        "w1": c(L0_ffn_W1).reshape(NEXP * D, DFF),
        "b1": c(L0_ffn_b1).reshape(NEXP * 32, 128),
        "w2": c(L0_ffn_W2).reshape(NEXP * DFF, D),
        "b2": c(L0_ffn_b2).reshape(NEXP, D),
        "eyebf": eyebf, "eyef": eyef, "iotap": iotap, "rowc": rowc,
    }
    in_maps = []
    for core in range(N_CORES):
        m = dict(common)
        m["x"] = np.ascontiguousarray(x[core])
        in_maps.append(m)
    return in_maps


def kernel(**inputs):
    nc = _get_nc()
    in_maps = _stage_inputs(**inputs)
    res = run_bass_kernel_spmd(nc, in_maps, core_ids=list(range(N_CORES)))
    out = np.stack([np.asarray(r["out"], dtype=np.float32) for r in res.results], axis=0)
    return out


# version 18
# speedup vs baseline: 1.0708x; 1.0708x over previous
import contextlib
import os
import numpy as np
import ml_dtypes

import concourse.bass as bass
import concourse.mybir as mybir
from concourse.bass import IndirectOffsetOnAxis
from concourse.bass_utils import run_bass_kernel_spmd

F32 = mybir.dt.float32
BF16 = mybir.dt.bfloat16
I32 = mybir.dt.int32
AF = mybir.ActivationFunctionType
ALU = mybir.AluOpType
AXX = mybir.AxisListType.X

N_CORES = 8
SEQ = 8192
D = 1024
H = 16
FD = 256
DFF = 4096
NEXP = 8
TG_A = 512
TG_B = 256
TG_C = 256

STREAMS = ("pe", "act", "dve", "pool", "sp")


class Plan:

    DMA_SLOTS = 8

    def __init__(self):
        self.steps = []
        self.by_stream = {s: [] for s in STREAMS}
        self.dma_seq = {s: [] for s in STREAMS}

    def add(self, stream, fn, deps=(), dma=False, tag=""):
        sid = len(self.steps)
        deps = tuple(d for d in deps if d is not None)
        if dma:
            qi = len(self.dma_seq[stream])
            sem_key = f"{stream}.dma{qi % self.DMA_SLOTS}"
            if qi >= self.DMA_SLOTS:
                deps = deps + (self.dma_seq[stream][qi - self.DMA_SLOTS],)
            self.dma_seq[stream].append(sid)
        else:
            sem_key = stream
        self.steps.append((stream, sem_key, 16 if dma else 1, fn, deps, tag))
        self.by_stream[stream].append(sid)
        return sid

    def emit(self, nc, block):
        n = len(self.steps)
        needed = [False] * n
        for (_, _, _, _, deps, _) in self.steps:
            for d in deps:
                needed[d] = True
        for sid, (_, key, _, _, _, _) in enumerate(self.steps):
            if ".dma" in key:
                needed[sid] = True
        counts = {}
        count_after = [None] * n
        for sid, (_, key, inc, _, _, _) in enumerate(self.steps):
            if needed[sid]:
                counts[key] = counts.get(key, 0) + inc
                count_after[sid] = counts[key]
        ctx = contextlib.ExitStack()
        sems = {k: ctx.enter_context(nc.semaphore("s_" + k.replace(".", "_")))
                for k in sorted(counts)}

        def make_body(stream):
            def body(eng):
                waited = {}
                for sid in self.by_stream[stream]:
                    (_, skey, inc, fn, deps, tag) = self.steps[sid]
                    for d in deps:
                        dkey = self.steps[d][1]
                        ca = count_after[d]
                        if waited.get(dkey, 0) < ca:
                            eng.wait_ge(sems[dkey], ca)
                            waited[dkey] = ca
                    ins = fn(eng)
                    if needed[sid]:
                        assert ins is not None, f"step {sid} [{tag}] returned None"
                        ins.then_inc(sems[skey], inc)
            return body

        block.sync(make_body("sp"))
        block.scalar(make_body("act"))
        block.vector(make_body("dve"))
        block.tensor(make_body("pe"))
        block.gpsimd(make_body("pool"))
        return ctx


def build_program(stage=3):
    nc = bass.Bass("TRN2", target_bir_lowering=False)

    x_d = nc.dram_tensor("x", [SEQ, D], F32, kind="ExternalInput")
    wq_d = nc.dram_tensor("wq", [D, D], F32, kind="ExternalInput")
    wk_d = nc.dram_tensor("wk", [D, D], F32, kind="ExternalInput")
    wv_d = nc.dram_tensor("wv", [D, D], F32, kind="ExternalInput")
    wp_d = nc.dram_tensor("wp", [D, D], F32, kind="ExternalInput")
    bq_d = nc.dram_tensor("bq", [D], F32, kind="ExternalInput")
    bk_d = nc.dram_tensor("bk", [D], F32, kind="ExternalInput")
    bvr_d = nc.dram_tensor("bvrep", [128, H * 65], F32, kind="ExternalInput")
    bp_d = nc.dram_tensor("bproj", [1, D], F32, kind="ExternalInput")
    wqf_d = nc.dram_tensor("wqf", [128, FD], F32, kind="ExternalInput")
    wkf_d = nc.dram_tensor("wkf", [128, FD], F32, kind="ExternalInput")
    wrt_d = nc.dram_tensor("wrt", [NEXP, D], F32, kind="ExternalInput")
    br_d = nc.dram_tensor("br", [1, NEXP], F32, kind="ExternalInput")
    w1_d = nc.dram_tensor("w1", [NEXP * D, DFF], F32, kind="ExternalInput")
    b1_d = nc.dram_tensor("b1", [NEXP * 32, 128], F32, kind="ExternalInput")
    w2_d = nc.dram_tensor("w2", [NEXP * DFF, D], F32, kind="ExternalInput")
    b2_d = nc.dram_tensor("b2", [NEXP, D], F32, kind="ExternalInput")
    eye_d = nc.dram_tensor("eyebf", [128, 128], BF16, kind="ExternalInput")
    eyef_d = nc.dram_tensor("eyef", [128, 128], F32, kind="ExternalInput")
    iota_d = nc.dram_tensor("iotap", [128, 1], F32, kind="ExternalInput")
    rowc_d = nc.dram_tensor("rowc", [1, 16], F32, kind="ExternalInput")
    out_d = nc.dram_tensor("out", [SEQ, D], F32, kind="ExternalOutput")
    x1_scr = nc.dram_tensor("x1scr", [SEQ, D], F32, kind="Internal")
    yt_scr = nc.dram_tensor("ytscr", [D, SEQ], BF16, kind="Internal")

    ctx = contextlib.ExitStack()

    def sb(name, shape, dt):
        return ctx.enter_context(nc.sbuf_tensor(name, shape, dt))

    selw1 = sb("selw1", [128, 32768], BF16)
    selw2 = sb("selw2", [128, 32768], BF16)
    warena = sb("warena", [128, 16384], BF16)
    a3 = sb("a3", [128, 13376], BF16)
    d4 = sb("d4", [128, 4096], BF16)

    ctxT = sb("ctxT", [128, H * 2 * 65], BF16)
    wqf = sb("wqf_s", [128, FD], BF16)
    wkf = sb("wkf_s", [128, FD], BF16)
    eye = sb("eye", [128, 128], BF16)
    eyef = sb("eyef_s", [128, 128], F32)
    bqc = sb("bqc", [128, 8], F32)
    bkc = sb("bkc", [128, 8], F32)
    bprow = sb("bprow_s", [1, D], BF16)
    ones_row = sb("ones_row", [1, 128], BF16)
    ones_col = sb("ones_col", [128, 1], BF16)
    iotap = sb("iotap_s", [128, 1], F32)
    rowc = sb("rowc_s", [1, 16], F32)
    b2row = sb("b2row", [1, D], BF16)
    dsc = sb("dsc", [128, 64], F32)
    b1selT = sb("b1selT", [128, 32], F32)
    idx1 = sb("idx1_s", [128, 1], I32)
    idx2 = sb("idx2_s", [128, 1], I32)
    idxb1 = sb("idxb1", [32, 1], I32)
    idxb2 = sb("idxb2", [2, 1], I32)
    idxf = sb("idxf", [128, 4], F32)
    erep = sb("erep", [128, 1], F32)
    ebf = sb("ebf", [1, 1], BF16)
    cfix = sb("cfix", [128, 130], F32)
    ones8f = sb("ones8f", [1, 8], F32)
    LNS = []
    for si in range(2):
        LNS.append({k: sb(f"ln{k}{si}", [128, 4], F32)
                    for k in ("sum", "sq", "mean", "var", "var2", "rstd", "nmr")})
    epscol = sb("epscol", [128, 1], F32)

    def fv(t, off, p, n):
        return t[0:p, off // 2: off // 2 + 2 * n].bitcast(F32)

    def bvw(t, off, p, n):
        return t[0:p, off // 2: off // 2 + n]

    x_tile = [fv(selw2, 0, 128, 4096), fv(selw2, 16384, 128, 4096)]
    x_tileB = [fv(selw2, 8192 * i, 128, 2048) for i in range(4)]
    x1sb_bufs = [fv(selw2, 32768, 128, 2048), fv(selw2, 53248, 128, 2048)]
    ysb_bufs = [bvw(selw2, 40960, 128, 2048), bvw(selw2, 49152, 128, 2048)]
    prodrow = fv(selw2, 45056, 1, 1024)
    w2sel = selw2
    ctxacc = [fv(selw1, 0, 65, H * FD), fv(selw1, 16384, 65, H * FD)]
    rows = fv(selw1, 0, 1, 2048)
    rowbuf = fv(selw1, 8192, 2, 1024)
    wr8_sb = fv(selw1, 16384, 8, 1024)
    prod8 = fv(selw1, 20608, 8, 1040)
    kp_bufs = [bvw(selw1, 53376, 128, 1024), bvw(selw1, 55424, 128, 1024)]
    bvrep = fv(selw1, 32768, 128, H * 65)
    lnjunk = bvw(selw1, 36928, 128, 4096)
    xn_sb = bvw(selw1, 45184, 128, 4096)
    w1sel = selw1
    wk_sb = bvw(warena, 0, 128, 8192)
    wv_sb = bvw(warena, 16384, 128, 8192)
    wq_sb = wk_sb
    wp_sb = wv_sb
    x1_tile = [fv(warena, 0, 128, 2048), fv(warena, 8192, 128, 2048)]
    xnT_A = bvw(a3, 0, 128, 4096)
    kT_A = bvw(a3, 8192, 128, 4096)
    v_A = bvw(a3, 16384, 128, H * 65 * 4)
    kp_A = bvw(a3, 24704, 128, 1024)
    xnT_B = bvw(a3, 0, 128, 2048)
    qT_B = bvw(a3, 4096, 128, 2048)
    qpT_B = [bvw(a3, 8192, 128, 512), bvw(a3, 9216, 128, 512)]
    atok_B = [bvw(a3, 10240, 128, 1024), bvw(a3, 12288, 128, 1024)]
    attT_B = [bvw(a3, 14336, 128, 1024), bvw(a3, 16384, 128, 1024)]
    h_C = bvw(a3, 0, 128, 8192)
    yT_C = [bvw(a3, 16384, 128, 2048), bvw(a3, 20480, 128, 2048)]
    ytout_B = [bvw(d4, 0, 128, 2048), bvw(d4, 4096, 128, 2048)]
    out_C = fv(d4, 0, 128, 2048)

    PX = ctx.enter_context(nc.psum_tensor("px", [128, 1024], F32))
    PV = ctx.enter_context(nc.psum_tensor("pv", [128, 1024], F32))
    PK = ctx.enter_context(nc.psum_tensor("pk", [128, 1024], F32))
    PC = ctx.enter_context(nc.psum_tensor("pc", [128, 512], F32))
    PS = ctx.enter_context(nc.psum_tensor("ps", [128, 512], F32))
    PXb = PX[:, :].bitcast(BF16)

    P = Plan()

    def dma(stream, out, in_, deps=(), tag="", slow=False):
        return P.add(stream,
                     lambda e, out=out, in_=in_: e.dma_start(
                         out=out, in_=in_, allow_slow_non_contiguous=slow),
                     deps=deps, dma=True, tag=tag)

    def mset(stream, ap, val, tag=""):
        return P.add(stream, lambda e, ap=ap, val=val: e.memset(ap, val), tag=tag)

    s_eye = dma("sp", eye[:], eye_d[:, :], tag="eye")
    s_eyef = dma("sp", eyef[:], eyef_d[:, :], tag="eyef")
    s_iota = dma("sp", iotap[:], iota_d[:, :], tag="iota")
    s_rowc = dma("sp", rowc[0:1, :], rowc_d[:, :], tag="rowc")
    s_bq = dma("sp", bqc[:], bq_d.rearrange("(c p) -> p c", p=128), tag="bq", slow=True)
    s_bk = dma("sp", bkc[:], bk_d.rearrange("(c p) -> p c", p=128), tag="bk", slow=True)
    s_bvr = dma("sp", bvrep, bvr_d[:, :], tag="bvrep")
    s_wqf = dma("pool", wqf[:], wqf_d[:, :], tag="wqf")
    s_wkf = dma("pool", wkf[:], wkf_d[:, :], tag="wkf")
    s_bpr = dma("pool", bprow[0:1, :], bp_d[:, :], tag="bprow")
    s_wk = dma("pool", wk_sb.rearrange("p (dc c) -> p dc c", c=D),
               wk_d.rearrange("(dc p) c -> p dc c", p=128), tag="wk")
    s_wv = dma("pool", wv_sb.rearrange("p (dc c) -> p dc c", c=D),
               wv_d.rearrange("(dc p) c -> p dc c", p=128), tag="wv")
    s_ones = mset("dve", ones_row[0:1, :], 1.0, tag="ones_row")
    s_onesc = mset("dve", ones_col[:, :], 1.0, tag="ones_col")
    s_vones = mset("dve", v_A.rearrange("p (s f) -> p s f", f=65)[:, :, 64:65], 1.0,
                   tag="v_ones")
    s_eps = mset("dve", epscol[:], 1e-5, tag="eps")
    s_ones8 = mset("dve", ones8f[0:1, :], 1.0, tag="ones8")
    s_pad8 = mset("dve", prod8[0:8, 1024:1032], 0.0, tag="pad8")

    _ln_chain = [[], []]

    def ln_stats(x_ap, nsub, deps, pfx, si=0):
        lnsum, lnsq, lnmean, lnvar, lnvar2 = (
            LNS[si]["sum"], LNS[si]["sq"], LNS[si]["mean"], LNS[si]["var"],
            LNS[si]["var2"])
        deps = list(deps) + list(_ln_chain[si])
        xv = x_ap.rearrange("p (s d) -> p s d", d=D)
        s_red = P.add("dve", lambda e: e.reduce_sum(lnsum[:, 0:nsub], xv, axis=AXX),
                      deps=deps, tag=pfx + ".red")
        sqs = []
        for s in range(nsub):
            sqs.append(P.add(
                "dve",
                lambda e, s=s: e.scalar_tensor_tensor(
                    lnjunk[:, s * D:(s + 1) * D], x_ap[:, s * D:(s + 1) * D], 1.0,
                    x_ap[:, s * D:(s + 1) * D], ALU.mult, ALU.mult,
                    accum_out=lnsq[:, s:s + 1]),
                deps=list(deps) if s == 0 else [], tag=pfx + f".sq{s}"))
        s_mean = P.add("dve", lambda e: e.tensor_scalar(lnmean[:, 0:nsub], lnsum[:, 0:nsub],
                                                        1.0 / D, None, ALU.mult),
                       deps=[s_red], tag=pfx + ".mean")
        s_v0 = P.add("dve", lambda e: e.tensor_tensor(lnvar[:, 0:nsub], lnmean[:, 0:nsub],
                                                      lnmean[:, 0:nsub], ALU.mult),
                     deps=[s_mean], tag=pfx + ".v0")
        s_v1 = P.add("dve", lambda e: e.tensor_scalar(lnvar2[:, 0:nsub], lnsq[:, 0:nsub],
                                                      1.0 / D, None, ALU.mult),
                     deps=[sqs[-1]], tag=pfx + ".v1")
        s_v2 = P.add("dve", lambda e: e.tensor_tensor(lnvar[:, 0:nsub], lnvar2[:, 0:nsub],
                                                      lnvar[:, 0:nsub], ALU.subtract),
                     deps=[s_v0, s_v1], tag=pfx + ".v2")
        return s_v2, s_red

    def ln_apply(x_ap, nsub, xn_out, pfx, s_v2, si=0, extra=()):
        lnmean, lnvar, lnvar2, lnrstd, lnnmr = (
            LNS[si]["mean"], LNS[si]["var"], LNS[si]["var2"], LNS[si]["rstd"],
            LNS[si]["nmr"])
        s_lg = P.add("act", lambda e: e.activation(lnvar2[:, 0:nsub], lnvar[:, 0:nsub],
                                                   AF.Ln, bias=epscol[:], scale=1.0),
                     deps=[s_v2, s_eps] + list(extra), tag=pfx + ".ln")
        s_rstd = P.add("act", lambda e: e.activation(lnrstd[:, 0:nsub], lnvar2[:, 0:nsub],
                                                     AF.Exp, bias=0.0, scale=-0.5),
                       deps=[s_lg], tag=pfx + ".rstd")
        s_nm0 = P.add("dve", lambda e: e.tensor_tensor(lnvar2[:, 0:nsub], lnmean[:, 0:nsub],
                                                       lnrstd[:, 0:nsub], ALU.mult),
                      deps=[s_rstd], tag=pfx + ".nm0")
        s_nmr = P.add("dve", lambda e: e.tensor_scalar(lnnmr[:, 0:nsub], lnvar2[:, 0:nsub],
                                                       -1.0, None, ALU.mult),
                      deps=[s_nm0], tag=pfx + ".nmr")
        affs = []
        for s in range(nsub):
            affs.append(P.add(
                "dve",
                lambda e, s=s: e.tensor_scalar(xn_out[:, s * D:(s + 1) * D],
                                               x_ap[:, s * D:(s + 1) * D],
                                               lnrstd[:, s:s + 1], lnnmr[:, s:s + 1],
                                               ALU.mult, ALU.add),
                deps=[s_nmr], tag=pfx + f".aff{s}"))
        _ln_chain[si][:] = affs
        return affs

    def ln_steps(x_ap, nsub, xn_out, deps, pfx, si=0):
        s_v2, s_red = ln_stats(x_ap, nsub, deps, pfx, si)
        affs = ln_apply(x_ap, nsub, xn_out, pfx, s_v2, si)
        return affs, s_red

    def tp_steps(src_bf, dst_bf, nsub, T, deps, pfx, extra_guard=None):
        evs = []
        for s in range(nsub):
            half = s % 2
            pb = PXb[:, half * 1024:(half + 1) * 1024]
            if len(evs) >= 2:
                guard = [evs[-2]]
            else:
                guard = [extra_guard] if extra_guard is not None else []
            tps = []
            for c in range(8):
                tps.append(P.add(
                    "pe",
                    lambda e, s=s, c=c, pb=pb: e.transpose(
                        pb[:, c * 128:(c + 1) * 128],
                        src_bf[:, s * D + c * 128: s * D + (c + 1) * 128], eye[:]),
                    deps=(list(deps) + [s_eye] + guard) if c == 0 else [],
                    tag=pfx + f".tp{s}.{c}"))
            evs.append(P.add(
                "dve",
                lambda e, s=s, pb=pb: e.tensor_copy(
                    dst_bf.rearrange("p (c t) -> p c t", t=T)[:, :, s * 128:(s + 1) * 128],
                    pb.rearrange("p (c t) -> p c t", t=128)),
                deps=[tps[-1]], tag=pfx + f".tpev{s}"))
        return evs

    nt_a = SEQ // TG_A
    xbuf_guard = {0: [], 1: []}
    ctx_ev = {}
    last_ctx_ev = [None, None]
    last_exp = {}
    prev_ln = []
    prev_trs = None
    prev_kev = None
    prev_vev = None
    last_kmm = None
    last_vev = None

    def emitA_p1a(t):
        buf = t % 2
        xa = x_tile[buf]
        s_ldx = dma("sp", xa.rearrange("p (s d) -> p s d", d=D),
                    x_d[t * TG_A:(t + 1) * TG_A, :].rearrange("(s p) d -> p s d", p=128),
                    deps=xbuf_guard[buf], tag=f"A{t}.ldx")
        xbuf_guard[buf] = []
        ln_deps = [s_ldx] + prev_ln + ([prev_trs] if prev_trs is not None else [])
        s_v2, s_red = ln_stats(xa, 4, ln_deps, f"A{t}.ln")
        xbuf_guard[buf].append(s_red)
        return xa, s_v2

    def emitA_p1c(t, xa, s_v2):
        nonlocal prev_ln
        affs = ln_apply(xa, 4, xn_sb, f"A{t}.ln", s_v2, extra=[prev_trs])
        prev_ln = affs
        xbuf_guard[t % 2].append(affs[-1])
        return affs

    def emitA_p1b(t, affs):
        nonlocal prev_trs
        trs = tp_steps(xn_sb, xnT_A, 4, TG_A, affs, f"A{t}", extra_guard=prev_kev)
        prev_trs = trs[-1]
        return trs

    def emitA_p2(t, trs):
        nonlocal prev_kev, prev_vev, last_kmm, last_vev
        kevs = []
        kmms = []
        for c in range(8):
            half = c % 2
            pb = PX[:, half * 512:(half + 1) * 512]
            mdeps = [trs[-1], s_wk]
            if len(kevs) >= 2:
                mdeps.append(kevs[-2])
            elif prev_kev is not None:
                mdeps.append(prev_kev)

            def kmm(e, c=c, pb=pb):
                ins = None
                wkv = wk_sb.rearrange("p (dc c) -> p dc c", c=D)
                xv = xnT_A.rearrange("p (dc t) -> p dc t", t=TG_A)
                for dc in range(8):
                    ins = e.matmul(pb, wkv[:, dc, c * 128:(c + 1) * 128], xv[:, dc, :],
                                   start=(dc == 0), stop=(dc == 7))
                return ins
            kmms.append(P.add("pe", kmm, deps=mdeps, tag=f"A{t}.kmm{c}"))
            kevs.append(P.add(
                "act",
                lambda e, c=c, pb=pb: e.activation(kT_A[:, c * TG_A:(c + 1) * TG_A], pb,
                                                   AF.Identity, bias=bkc[:, c:c + 1],
                                                   scale=1.0),
                deps=[kmms[-1], s_bk], tag=f"A{t}.kev{c}"))
        last_kmm = kmms[-1]
        prev_kev = kevs[-1]
        vevs = []
        for s in range(4):
            vdeps = [trs[-1], s_wv]
            if vevs:
                vdeps.append(vevs[-1])
            else:
                vdeps.append(s_vones)
                if prev_vev is not None:
                    vdeps.append(prev_vev)
                vdeps.append(last_exp.get(("bank", 1)))

            def vmm(e, s=s):
                ins = None
                xv = xnT_A.rearrange("p (dc t) -> p dc t", t=TG_A)
                wvv = wv_sb.rearrange("p (dc c) -> p dc c", c=D)
                for half in range(2):
                    pb = PV[:, half * 512:(half + 1) * 512]
                    for dc in range(8):
                        ins = e.matmul(pb, xv[:, dc, s * 128:(s + 1) * 128],
                                       wvv[:, dc, half * 512:(half + 1) * 512],
                                       start=(dc == 0), stop=(dc == 7))
                return ins
            s_vmm = P.add("pe", vmm, deps=vdeps, tag=f"A{t}.vmm{s}")

            def vev(e, s=s):
                dv = v_A.rearrange("p (ss hh f) -> p ss hh f", hh=16, f=65)[:, s, :, 0:64]
                e.tensor_copy(dv[:, 0:8, :],
                              PV[:, 0:512].rearrange("p (hh f) -> p hh f", f=64))
                return e.tensor_copy(dv[:, 8:16, :],
                                     PV[:, 512:1024].rearrange("p (hh f) -> p hh f", f=64))
            vevs.append(P.add("dve", vev, deps=[s_vmm], tag=f"A{t}.vev{s}"))
        prev_vev = vevs[-1]
        last_vev = vevs[-1]
        return kevs, vevs

    def emitA_p3(t, kevs, vevs):
        kTv = kT_A.rearrange("p (c t) -> p c t", t=TG_A)
        vv = v_A.rearrange("p (ss hh f) -> p ss hh f", hh=16, f=65)
        exps = {}

        def emit_kq(h):
            par = h % 2
            bank = PK if par == 0 else PV
            kdeps = [kevs[h // 2], s_wkf]
            g = last_exp.get(("bank", par))
            if g is not None:
                kdeps.append(g)
            if par == 1 and h == 1:
                kdeps.append(vevs[-1])

            def kqmm(e, h=h, par=par, bank=bank):
                ins = None
                for s in range(4):
                    ins = e.matmul(bank[:, s * 256:(s + 1) * 256],
                                   kTv[64 * par:64 * par + 64, h // 2,
                                       s * 128:(s + 1) * 128],
                                   wkf[64 * par:64 * par + 64, :],
                                   start=True, stop=True)
                return ins
            s_kq = P.add("pe", kqmm, deps=kdeps, tag=f"A{t}.kq{h}")
            edeps = [s_kq]
            g2 = last_exp.get(("kp", par))
            if g2 is not None:
                edeps.append(g2)
            s_exp = P.add("act",
                          lambda e, par=par, bank=bank: e.activation(
                              kp_bufs[par][:, :], bank[:, :], AF.Exp),
                          deps=edeps, tag=f"A{t}.exp{h}")
            exps[h] = s_exp
            last_exp[("bank", par)] = s_exp

        emit_kq(0)
        emit_kq(1)
        for h in range(H):
            par = h % 2
            cdeps = [exps[h], vevs[3]]
            if last_ctx_ev[par] is not None:
                cdeps.append(last_ctx_ev[par])

            def ctxmm(e, h=h, par=par):
                ins = None
                cslot = PC[0:65, 0:256] if par == 0 else PS[0:65, 0:256]
                for s in range(4):
                    ins = e.matmul(cslot, vv[:, s, h, :],
                                   kp_bufs[par][:, s * 256:(s + 1) * 256],
                                   start=(s == 0), stop=(s == 3))
                return ins
            s_ctx = P.add("pe", ctxmm, deps=cdeps, tag=f"A{t}.ctx{h}")
            last_exp[("kp", par)] = s_ctx

            def ctxev(e, h=h, par=par, t=t):
                dst = ctxacc[(t + 1) % 2][:, h * FD:(h + 1) * FD]
                srcp = PC[0:65, 0:256] if par == 0 else PS[0:65, 0:256]
                if t == 0:
                    return e.tensor_copy(dst, srcp)
                return e.scalar_tensor_tensor(dst, srcp, 1.0,
                                              ctxacc[t % 2][:, h * FD:(h + 1) * FD],
                                              ALU.mult, ALU.add)
            ev_deps = [s_ctx]
            if h in ctx_ev:
                ev_deps.append(ctx_ev[h])
            s_cev = P.add("dve", ctxev, deps=ev_deps, tag=f"A{t}.ctxev{h}")
            ctx_ev[h] = s_cev
            last_ctx_ev[par] = s_cev
            if h + 2 < H:
                emit_kq(h + 2)

    st0 = emitA_p1a(0)
    affs_q = {0: emitA_p1c(0, *st0)}
    trs0 = emitA_p1b(0, affs_q[0])
    parts_c = emitA_p2(0, trs0)
    st_q = {1: emitA_p1a(1)}
    affs_q[1] = emitA_p1c(1, *st_q[1])
    for t in range(nt_a):
        if t + 1 < nt_a:
            trs_n = emitA_p1b(t + 1, affs_q[t + 1])
        if t + 2 < nt_a:
            st_q[t + 2] = emitA_p1a(t + 2)
        emitA_p3(t, *parts_c)
        if t + 2 < nt_a:
            affs_q[t + 2] = emitA_p1c(t + 2, *st_q[t + 2])
        if t + 1 < nt_a:
            parts_c = emitA_p2(t + 1, trs_n)

    ctxfin = []
    ctx_final_buf = ctxacc[nt_a % 2]
    for h in range(H):
        for fc in range(2):
            s_t = P.add(
                "pe",
                lambda e, h=h, fc=fc: e.transpose(
                    PS[0:128, 0:65],
                    ctx_final_buf[0:65, h * FD + fc * 128: h * FD + (fc + 1) * 128],
                    eyef[0:65, 0:65]),
                deps=[ctx_ev[h], ctx_ev[H - 1], ctx_ev[H - 2], s_eyef] +
                     ([ctxfin[-1]] if ctxfin else []),
                tag=f"ctf{h}.{fc}")
            s_c0 = P.add("dve", lambda e: e.tensor_copy(cfix[:, 0:65], PS[0:128, 0:65]),
                         deps=[s_t], tag=f"ctf{h}.{fc}.c0")
            s_c1 = P.add("dve",
                         lambda e, h=h: e.tensor_scalar(cfix[:, 65:130],
                                                        bvrep[:, h * 65:(h + 1) * 65],
                                                        cfix[:, 64:65], None, ALU.mult),
                         deps=[s_c0, s_bvr], tag=f"ctf{h}.{fc}.c1")
            s_c2 = P.add("dve",
                         lambda e, h=h, fc=fc: e.tensor_tensor(
                             ctxT[:, (h * 2 + fc) * 65:(h * 2 + fc + 1) * 65],
                             cfix[:, 0:65], cfix[:, 65:130], ALU.add),
                         deps=[s_c1], tag=f"ctf{h}.{fc}.c2")
            ctxfin.append(s_c2)

    if stage == 1:
        dumps = []
        if os.environ.get("ATT_DUMP_V"):
            s_cv = P.add("act", lambda e: e.activation(x_tile[0], v_A[:, 0:4096],
                                                       AF.Copy),
                         deps=[ctxfin[-1]], tag="dumpv.c")
            for i in range(4):
                dumps.append(dma("sp", out_d[4096 + 128 * i:4096 + 128 * (i + 1), :],
                                 x_tile[0][:, 1024 * i:1024 * (i + 1)],
                                 deps=[s_cv], tag=f"dumpv{i}"))
            s_ck = P.add("act", lambda e: e.activation(x_tile[1], kp_A[:, 0:1024]
                                                       if False else kT_A[:, 0:4096],
                                                       AF.Copy),
                         deps=[ctxfin[-1]], tag="dumpk.c")
            for i in range(4):
                dumps.append(dma("sp", out_d[5120 + 128 * i:5120 + 128 * (i + 1), :],
                                 x_tile[1][:, 1024 * i:1024 * (i + 1)],
                                 deps=[s_ck], tag=f"dumpk{i}"))
        for i in range(4):
            dumps.append(dma("sp", out_d[65 * i:65 * i + 65, :],
                             ctx_final_buf[0:65, i * 1024:(i + 1) * 1024],
                             deps=[ctxfin[-1]], tag=f"dump{i}"))
        P.add("sp", lambda e: None, deps=dumps, tag="final_wait")
        with nc.Block() as block:
            emctx = P.emit(nc, block)
        emctx.close()
        ctx.close()
        return nc

    s_wq = dma("pool", wq_sb.rearrange("p (dc c) -> p dc c", c=D),
               wq_d.rearrange("(dc p) c -> p dc c", p=128),
               deps=[last_kmm, prev_kev], tag="wq")
    s_wp = dma("pool", wp_sb.rearrange("p (dc c) -> p dc c", c=D),
               wp_d.rearrange("(dc p) c -> p dc c", p=128),
               deps=[last_vev], tag="wp")

    nt_b = SEQ // TG_B
    xbuf_guard = {0: list(xbuf_guard[0]), 1: list(xbuf_guard[0]),
                  2: list(xbuf_guard[1]), 3: list(xbuf_guard[1])}
    ysum_ev = None
    prev_ln_b = list(prev_ln)
    prev_trs_b = prev_trs
    prev_qev = prev_kev
    last_ps_guard = None
    kq_guard = {0: None, 1: None}
    last_x1 = {0: None, 1: None}
    last_yred = {0: None, 1: None}
    x1c_steps = {}
    last_x1st = None
    last_yst = None
    last_attT = {0: None, 1: None}
    last_mults = {0: None, 1: None}
    last_pc_guard = None
    last_ysmm = None
    x1_store_steps = []
    yt_store_steps = []
    last_qp = [None, None]
    B_last_proj = None
    B_last_qmm = None
    prev_ytev = None

    ctv = ctxT.rearrange("p (s f) -> p s f", f=65)

    def slot_ap(st, h):
        if h < 7:
            base = PV if st == 0 else PK
            return base[0:128, h * 65: h * 65 + 65]
        if h < 14:
            base = PV if st == 0 else PK
            return base[0:128, 512 + (h - 7) * 65: 512 + (h - 7) * 65 + 65]
        off = (0 if st == 0 else 256) + (h - 14) * 65
        return PC[0:128, off: off + 65]

    def emitB_p1a(t):
        buf = t % 4
        xa = x_tileB[buf]
        s_ldx = dma("sp", xa.rearrange("p (s d) -> p s d", d=D),
                    x_d[t * TG_B:(t + 1) * TG_B, :].rearrange("(s p) d -> p s d", p=128),
                    deps=xbuf_guard[buf], tag=f"B{t}.ldx")
        xbuf_guard[buf] = []
        ln_deps = [s_ldx] + prev_ln_b + ([prev_trs_b] if prev_trs_b is not None else [])
        s_v2, s_red = ln_stats(xa, 2, ln_deps, f"B{t}.ln")
        xbuf_guard[buf].append(s_red)
        return xa, s_v2

    def emitB_p1c(t, xa, s_v2):
        nonlocal prev_ln_b
        affs = ln_apply(xa, 2, xn_sb, f"B{t}.ln", s_v2, extra=[prev_trs_b])
        prev_ln_b = affs
        xbuf_guard[t % 4].append(affs[-1])
        return affs

    def emitB_p1b(t, affs, qev_guard):
        nonlocal prev_trs_b
        trs = tp_steps(xn_sb, xnT_B, 2, TG_B,
                       affs + [prev_ytev, last_attT[0], last_attT[1], kq_guard[1]],
                       f"B{t}", extra_guard=qev_guard)
        prev_trs_b = trs[-1]
        return trs

    def emitB_p2(t, trs):
        nonlocal prev_qev, B_last_qmm
        qevs = []
        for c in range(8):
            half = c % 2
            pb = PX[:, half * 512: half * 512 + 256]
            mdeps = [trs[-1], s_wq]
            if len(qevs) >= 2:
                mdeps.append(qevs[-2])
            else:
                if prev_qev is not None:
                    mdeps.append(prev_qev)
                mdeps += [last_attT[0], last_attT[1], prev_ytev, kq_guard[1]]

            def qmm(e, c=c, pb=pb):
                ins = None
                wqv = wq_sb.rearrange("p (dc c) -> p dc c", c=D)
                xv = xnT_B.rearrange("p (dc t) -> p dc t", t=TG_B)
                for dc in range(8):
                    ins = e.matmul(pb, wqv[:, dc, c * 128:(c + 1) * 128], xv[:, dc, :],
                                   start=(dc == 0), stop=(dc == 7))
                return ins
            s_qmm = P.add("pe", qmm, deps=mdeps, tag=f"B{t}.qmm{c}")
            B_last_qmm = s_qmm
            qevs.append(P.add(
                "act",
                lambda e, c=c, pb=pb: e.activation(qT_B[:, c * TG_B:(c + 1) * TG_B], pb,
                                                   AF.Identity, bias=bqc[:, c:c + 1],
                                                   scale=1.0),
                deps=[s_qmm, s_bq], tag=f"B{t}.qev{c}"))
        prev_qev = qevs[-1]
        return qevs

    def emitB_heads(t, xa, qevs):
        nonlocal last_ps_guard, last_pc_guard
        buf = t % 4
        qTv = qT_B.rearrange("p (c t) -> p c t", t=TG_B)
        einsum_last = {0: [], 1: []}
        exps = {}

        def emit_kq(h):
            par = h % 2
            kbank = PS[:, 0:512] if par == 0 else PX[:, 0:512]
            kdeps = [qevs[h // 2], s_wqf]
            g = kq_guard[par]
            if g is not None:
                kdeps.append(g)
            elif t == 0 and h == 0:
                kdeps.append(ctxfin[-1])
            if h <= 1:
                kdeps += [prev_ytev, qevs[-1]]

            def kqmm(e, h=h, kbank=kbank):
                ins = None
                for fh in range(2):
                    ins = e.matmul(kbank[:, fh * 256:(fh + 1) * 256],
                                   wqf[64 * (h % 2):64 * (h % 2) + 64,
                                       fh * 128:(fh + 1) * 128],
                                   qTv[64 * (h % 2):64 * (h % 2) + 64, h // 2, :],
                                   start=True, stop=True)
                return ins
            s_kq = P.add("pe", kqmm, deps=kdeps, tag=f"B{t}.kq{h}")
            edeps = [s_kq]
            if last_qp[par] is not None:
                edeps.append(last_qp[par])
            s_exp = P.add("act",
                          lambda e, par=par, kbank=kbank: e.activation(
                              qpT_B[par][:, :], kbank, AF.Exp),
                          deps=edeps, tag=f"B{t}.qexp{h}")
            kq_guard[par] = s_exp
            exps[h] = s_exp

        emit_kq(0)
        emit_kq(1)
        for h in range(H):
            par = h % 2
            s_exp = exps[h]
            for st in range(2):
                mdeps = [s_exp]
                if t == 0 and h == 0:
                    mdeps.append(ctxfin[-1])
                if h == 0:
                    mdeps += [last_mults[st], last_x1[st]]
                if h == 14 and last_pc_guard is not None:
                    mdeps.append(last_pc_guard)

                def emm(e, h=h, par=par, st=st):
                    ins = None
                    for fc in range(2):
                        ins = e.matmul(slot_ap(st, h),
                                       qpT_B[par][:, fc * 256 + st * 128:
                                                  fc * 256 + st * 128 + 128],
                                       ctv[:, h * 2 + fc, :],
                                       start=(fc == 0), stop=(fc == 1))
                    return ins
                s_e = P.add("pe", emm, deps=mdeps, tag=f"B{t}.ein{h}.{st}")
                einsum_last[st].append(s_e)
            last_qp[par] = einsum_last[1][-1]
            if h + 2 < H:
                emit_kq(h + 2)
        last_ps_guard = exps[H - 1]

        return einsum_last

    def emitB_tail(t, xa, qevs, einsum_last, qevs_next, trs_next):
        nonlocal last_x1st, B_last_proj
        proj_steps = []
        for st in range(2):
            base = PV if st == 0 else PK
            ein_done = [einsum_last[st][-1], einsum_last[1][-1]]

            def dext(e, st=st, base=base):
                e.tensor_scalar(dsc[:, st * 32 + 0:st * 32 + 7],
                                base[0:128, 0:455].rearrange(
                                    "p (s f) -> p s f", f=65)[:, :, 64],
                                1e-6, None, ALU.add)
                e.tensor_scalar(dsc[:, st * 32 + 7:st * 32 + 14],
                                base[0:128, 512:967].rearrange(
                                    "p (s f) -> p s f", f=65)[:, :, 64],
                                1e-6, None, ALU.add)
                off = 0 if st == 0 else 256
                return e.tensor_scalar(dsc[:, st * 32 + 14:st * 32 + 16],
                                       PC[0:128, off:off + 130].rearrange(
                                           "p (s f) -> p s f", f=65)[:, :, 64],
                                       1e-6, None, ALU.add)
            s_dx = P.add("dve", dext, deps=ein_done, tag=f"B{t}.dext{st}")
            s_dr = P.add("dve",
                         lambda e, st=st: e.reciprocal(dsc[:, st * 32 + 16:st * 32 + 32],
                                                       dsc[:, st * 32 + 0:st * 32 + 16]),
                         deps=[s_dx], tag=f"B{t}.drec{st}")
            mdeps0 = [s_dr, last_attT[st]]

            def dmulA(e, st=st, base=base):
                return e.tensor_tensor(
                    atok_B[st][:, 0:448].rearrange("p (s k) -> p s k", k=64),
                    base[0:128, 0:455].rearrange("p (s f) -> p s f", f=65)[:, :, 0:64],
                    dsc[:, st * 32 + 16:st * 32 + 23].to_broadcast([128, 7, 64]),
                    ALU.mult)

            def dmulB(e, st=st, base=base):
                return e.tensor_tensor(
                    atok_B[st][:, 448:896].rearrange("p (s k) -> p s k", k=64),
                    base[0:128, 512:967].rearrange("p (s f) -> p s f", f=65)[:, :, 0:64],
                    dsc[:, st * 32 + 23:st * 32 + 30].to_broadcast([128, 7, 64]),
                    ALU.mult)

            def dmulC(e, st=st):
                off = 0 if st == 0 else 256
                return e.tensor_tensor(
                    atok_B[st][:, 896:1024].rearrange("p (s k) -> p s k", k=64),
                    PC[0:128, off:off + 130].rearrange("p (s f) -> p s f", f=65)[:, :, 0:64],
                    dsc[:, st * 32 + 30:st * 32 + 32].to_broadcast([128, 2, 64]),
                    ALU.mult)
            mults = [P.add("dve", dmulA, deps=mdeps0, tag=f"B{t}.dmulA{st}"),
                     P.add("dve", dmulB, deps=[], tag=f"B{t}.dmulB{st}"),
                     P.add("dve", dmulC, deps=[], tag=f"B{t}.dmulC{st}")]
            last_mults[st] = mults[-1]

            pb = PXb[:, st * 1024:(st + 1) * 1024]
            tdeps = [mults[-1], qevs[-1], last_attT[st], prev_trs_b, prev_ytev]
            if qevs_next is not None:
                tdeps.append(qevs_next[-1])
            if trs_next is not None:
                tdeps.append(trs_next[-1])
            tps = []
            for c in range(8):
                tps.append(P.add(
                    "pe",
                    lambda e, c=c, st=st, pb=pb: e.transpose(
                        pb[:, c * 128:(c + 1) * 128],
                        atok_B[st][:, c * 128:(c + 1) * 128], eye[:]),
                    deps=tdeps if c == 0 else [], tag=f"B{t}.atp{st}.{c}"))
            s_aev = P.add("dve",
                          lambda e, st=st, pb=pb: e.tensor_copy(attT_B[st][:, :], pb),
                          deps=[tps[-1]], tag=f"B{t}.atpev{st}")
            last_attT[st] = s_aev

            pdeps = [s_aev, s_wp, s_bpr, s_ones, mults[-1]]

            def pmm(e, st=st, base=base):
                ins = None
                wpv = wp_sb.rearrange("p (dc c) -> p dc c", c=D)
                av = attT_B[st].rearrange("p (dc t) -> p dc t", t=128)
                for half in range(2):
                    pb2 = base[:, half * 512:(half + 1) * 512]
                    for dc in range(8):
                        ins = e.matmul(pb2, av[:, dc, :],
                                       wpv[:, dc, half * 512:(half + 1) * 512],
                                       start=(dc == 0), stop=False)
                    ins = e.matmul(pb2, ones_row[0:1, :],
                                   bprow[0:1, half * 512:(half + 1) * 512],
                                   start=False, stop=True)
                return ins
            s_proj = P.add("pe", pmm, deps=pdeps, tag=f"B{t}.proj{st}")
            proj_steps.append(s_proj)
            B_last_proj = s_proj

            xdeps = [s_proj, last_x1[st], last_x1st, last_yred[t % 2]]
            x1sb = x1sb_bufs[t % 2]
            buf = t % 4

            def x1c(e, st=st, base=base, xa=xa, x1sb=x1sb):
                e.scalar_tensor_tensor(x1sb[:, st * D:st * D + 512], base[:, 0:512], 1.0,
                                       xa[:, st * D:st * D + 512], ALU.mult, ALU.add)
                return e.scalar_tensor_tensor(x1sb[:, st * D + 512:(st + 1) * D],
                                              base[:, 512:1024], 1.0,
                                              xa[:, st * D + 512:(st + 1) * D],
                                              ALU.mult, ALU.add)
            s_x1 = P.add("dve", x1c, deps=xdeps, tag=f"B{t}.x1{st}")
            last_x1[st] = s_x1
            x1c_steps.setdefault(t, []).append(s_x1)
            xbuf_guard[t % 4].append(s_x1)

        s_x1st = dma("act", x1_scr[t * TG_B:(t + 1) * TG_B, :]
                     .rearrange("(s p) d -> p s d", p=128),
                     x1sb.rearrange("p (s d) -> p s d", d=D),
                     deps=[last_x1[0], last_x1[1]], tag=f"B{t}.x1st")
        last_x1st = s_x1st
        x1_store_steps.append(s_x1st)
        return proj_steps

    def emitB_p4bS(t):
        nonlocal last_ysmm
        x1sb = x1sb_bufs[t % 2]
        yln_deps = list(x1c_steps[t])
        if last_ysmm is not None:
            yln_deps.append(last_ysmm)
        s_v2y, s_yred = ln_stats(x1sb, 2, yln_deps, f"B{t}.yln", si=1)
        return s_v2y

    def emitB_p4bSa(t, s_v2y):
        x1sb = x1sb_bufs[t % 2]
        ysb = ysb_bufs[t % 2]
        yaffs = ln_apply(x1sb, 2, ysb, f"B{t}.yln", s_v2y, si=1)
        last_yred[t % 2] = yaffs[-1]
        return yaffs

    def emitB_p4b(t, yaffs, proj_steps, qevs_next):
        nonlocal last_ps_guard, last_pc_guard, last_ysmm, ysum_ev
        nonlocal last_yst, prev_ytev
        ysb = ysb_bufs[t % 2]
        ytdeps = list(yaffs) + [prev_trs_b]
        if last_yst is not None:
            ytdeps.append(last_yst)
        if qevs_next is not None:
            ytdeps.append(qevs_next[-1])
        ytevs = tp_steps(ysb, ytout_B[t % 2], 2, TG_B, ytdeps, f"B{t}.yt",
                         extra_guard=proj_steps[-1])
        prev_ytev = ytevs[-1]
        s_ytst = dma("act",
                     yt_scr[:, t * TG_B:(t + 1) * TG_B]
                     .rearrange("(dc p) t -> p dc t", p=128),
                     ytout_B[t % 2].rearrange("p (dc t) -> p dc t", t=TG_B),
                     deps=ytevs, tag=f"B{t}.ytst")
        last_yst = s_ytst
        yt_store_steps.append(s_ytst)

        ydeps = yaffs + [s_onesc, kq_guard[0], kq_guard[1]]

        def ysmmA(e):
            ins = None
            for st in range(2):
                ins = e.matmul(PS[0:1, 0:512], ones_col[:, :],
                               ysb[:, st * D: st * D + 512],
                               start=(st == 0), stop=(st == 1))
            return ins

        def ysmmB(e):
            ins = None
            for st in range(2):
                ins = e.matmul(PC[0:1, 0:512], ones_col[:, :],
                               ysb[:, st * D + 512: st * D + 1024],
                               start=(st == 0), stop=(st == 1))
            return ins
        s_ysA = P.add("pe", ysmmA, deps=ydeps, tag=f"B{t}.ysA")
        s_ysB = P.add("pe", ysmmB, deps=ydeps + [last_mults[0], last_mults[1]],
                      tag=f"B{t}.ysB")
        last_ysmm = s_ysB

        def ysev(e, t=t):
            src = rows[0:1, (t % 2) * 1024:(t % 2) * 1024 + 1024]
            dst = rows[0:1, ((t + 1) % 2) * 1024:((t + 1) % 2) * 1024 + 1024]
            if t == 0:
                e.tensor_copy(dst[0:1, 0:512], PS[0:1, 0:512])
                return e.tensor_copy(dst[0:1, 512:1024], PC[0:1, 0:512])
            e.scalar_tensor_tensor(dst[0:1, 0:512], PS[0:1, 0:512], 1.0,
                                   src[0:1, 0:512], ALU.mult, ALU.add)
            return e.scalar_tensor_tensor(dst[0:1, 512:1024], PC[0:1, 0:512], 1.0,
                                          src[0:1, 512:1024], ALU.mult, ALU.add)
        ysum_ev = P.add("dve", ysev, deps=[s_ysA, s_ysB] +
                        ([ysum_ev] if ysum_ev is not None else []), tag=f"B{t}.ysev")
        last_ps_guard = ysum_ev
        kq_guard[0] = ysum_ev
        last_pc_guard = ysum_ev

    stb = {0: emitB_p1a(0)}
    affb = {0: emitB_p1c(0, *stb[0])}
    trsb = emitB_p1b(0, affb[0], prev_qev)
    qevs_c = emitB_p2(0, trsb)
    stb[1] = emitB_p1a(1)
    affb[1] = emitB_p1c(1, *stb[1])
    xa_c = stb[0][0]
    pst_q = {}
    yaffs_q = {}
    for t in range(nt_b):
        if t >= 1:
            yaffs_q[t - 1] = emitB_p4bS(t - 1)
        if t + 2 < nt_b:
            stb[t + 2] = emitB_p1a(t + 2)
        eins = emitB_heads(t, xa_c, qevs_c)
        if t >= 1:
            yaffs_q[t - 1] = emitB_p4bSa(t - 1, yaffs_q[t - 1])
        if t + 1 < nt_b:
            trs_nb = emitB_p1b(t + 1, affb[t + 1], qevs_c[-1])
            qevs_n = emitB_p2(t + 1, trs_nb)
        else:
            trs_nb = None
            qevs_n = None
        pst_q[t] = emitB_tail(t, xa_c, qevs_c, eins, qevs_n, trs_nb)
        if t + 2 < nt_b:
            affb[t + 2] = emitB_p1c(t + 2, *stb[t + 2])
        if t >= 1:
            emitB_p4b(t - 1, yaffs_q[t - 1], pst_q[t - 1], qevs_n)
        if t + 1 < nt_b:
            qevs_c = qevs_n
            xa_c = stb[t + 1][0]
    v2l = emitB_p4bS(nt_b - 1)
    ya = emitB_p4bSa(nt_b - 1, v2l)
    emitB_p4b(nt_b - 1, ya, pst_q[nt_b - 1], None)

    ysum_final = rows[0:1, (nt_b % 2) * 1024:(nt_b % 2) * 1024 + 1024]

    if stage == 2:
        dumps = []
        for i in range(8):
            dumps.append(dma("sp", out_d[i * 1024:(i + 1) * 1024, :],
                             x1_scr[i * 1024:(i + 1) * 1024, :],
                             deps=[x1_store_steps[-1], ysum_ev], tag=f"dump{i}"))
        P.add("sp", lambda e: None, deps=dumps, tag="final_wait")
        with nc.Block() as block:
            emctx = P.emit(nc, block)
        emctx.close()
        ctx.close()
        return nc

    s_wr8 = dma("sp", cfix[0:8, 0:128].bitcast(BF16)[0:8, 0:256] if False else wr8_sb,
                wrt_d[:, :], tag="R.wr8")
    s_yb = P.add("pe", lambda e: (
        e.matmul(PV[0:8, 0:512], ones8f[0:1, :], ysum_final[0:1, 0:512],
                 start=True, stop=True),
        e.matmul(PK[0:8, 0:512], ones8f[0:1, :], ysum_final[0:1, 512:1024],
                 start=True, stop=True))[-1],
        deps=[ysum_ev, s_ones8, last_mults[0], last_mults[1], last_x1[0], last_x1[1]],
        tag="R.yb")
    s_prod = P.add("dve", lambda e: (
        e.tensor_tensor(prod8[0:8, 0:512], PV[0:8, 0:512], wr8_sb[0:8, 0:512],
                        ALU.mult),
        e.tensor_tensor(prod8[0:8, 512:1024], PK[0:8, 0:512], wr8_sb[0:8, 512:1024],
                        ALU.mult))[-1],
        deps=[s_yb, s_wr8], tag="R.prod")
    s_rsum = P.add("dve", lambda e: e.reduce_sum(prod8[0:8, 1024:1025],
                                                 prod8[0:8, 0:1024], axis=AXX),
                   deps=[s_prod], tag="R.rsum")
    s_sct = P.add("pe", lambda e: e.transpose(PS[0:8, 0:8], prod8[0:8, 1024:1032],
                                              eyef[0:8, 0:8]),
                  deps=[s_rsum, s_eyef, last_ps_guard, s_pad8], tag="R.sct")
    score_steps = [P.add("dve", lambda e: e.tensor_copy(dsc[0:1, 32:40],
                                                        PS[0:1, 0:8]),
                         deps=[s_sct], tag="R.scv")]
    ysum_pad = P.add("dve", lambda e: e.memset(prod8[0:8, 1025:1032], 0.0),
                     deps=[s_rsum], tag="R.pad") if False else None
    s_br = dma("sp", dsc[0:1, 40:48], br_d[:, :], deps=[ysum_ev], tag="R.br")
    s_sc = P.add("dve",
                 lambda e: e.tensor_scalar(dsc[0:1, 48:56], dsc[0:1, 32:40],
                                           1.0 / SEQ, None, ALU.mult),
                 deps=score_steps + [s_br], tag="R.scale")
    s_sc2 = P.add("dve",
                  lambda e: e.tensor_tensor(dsc[0:1, 0:8], dsc[0:1, 48:56],
                                            dsc[0:1, 40:48], ALU.add),
                  deps=[s_sc], tag="R.addbr")
    s_mx = P.add("dve", lambda e: e.reduce_max(dsc[0:1, 8:9], dsc[0:1, 0:8], axis=AXX),
                 deps=[s_sc2], tag="R.max")
    s_msk = P.add("dve",
                  lambda e: e.tensor_scalar(dsc[0:1, 16:24], dsc[0:1, 0:8],
                                            dsc[0:1, 8:9], None, ALU.is_ge),
                  deps=[s_mx], tag="R.mask")
    s_t0 = P.add("dve",
                 lambda e: e.tensor_scalar(dsc[0:1, 24:32], dsc[0:1, 16:24],
                                           -8.0, None, ALU.mult),
                 deps=[s_msk], tag="R.t0")
    s_t1 = P.add("dve",
                 lambda e: e.tensor_tensor(dsc[0:1, 16:24], dsc[0:1, 24:32],
                                           rowc[0:1, 0:8], ALU.add),
                 deps=[s_t0, s_rowc], tag="R.t1")
    s_emin = P.add("dve",
                   lambda e: e.tensor_reduce(dsc[0:1, 9:10], dsc[0:1, 16:24],
                                             AXX, ALU.min),
                   deps=[s_t1], tag="R.emin")
    s_ef = P.add("dve",
                 lambda e: e.tensor_scalar(dsc[0:1, 10:11], dsc[0:1, 9:10],
                                           0.0, None, ALU.add),
                 deps=[s_emin], tag="R.ef")
    s_ebf = P.add("act", lambda e: e.activation(ebf[0:1, :], dsc[0:1, 10:11], AF.Copy),
                  deps=[s_ef], tag="R.ebf")
    s_erepmm = P.add("pe",
                     lambda e: e.matmul(PS[0:128, 0:1], ones_row[0:1, :], ebf[0:1, :],
                                        start=True, stop=True),
                     deps=[s_ebf, s_ones, ysum_ev], tag="R.erep")
    s_erep = P.add("dve", lambda e: e.tensor_copy(erep[:], PS[0:128, 0:1]),
                   deps=[s_erepmm], tag="R.erepc")
    s_i1a = P.add("dve",
                  lambda e: e.tensor_scalar(idxf[:, 0:1], erep[:], 1024.0, None, ALU.mult),
                  deps=[s_erep], tag="R.i1a")
    s_i1b = P.add("dve",
                  lambda e: e.tensor_tensor(idxf[:, 1:2], idxf[:, 0:1], iotap[:], ALU.add),
                  deps=[s_i1a, s_iota], tag="R.i1b")
    s_i2a = P.add("dve",
                  lambda e: e.tensor_scalar(idxf[:, 2:3], erep[:], 4096.0, None, ALU.mult),
                  deps=[s_erep], tag="R.i2a")
    s_i2b = P.add("dve",
                  lambda e: e.tensor_tensor(idxf[:, 3:4], idxf[:, 2:3], iotap[:], ALU.add),
                  deps=[s_i2a, s_iota], tag="R.i2b")
    s_ib1a = P.add("dve",
                   lambda e: e.tensor_scalar(dsc[0:32, 32:33], erep[0:32, :], 32.0,
                                             None, ALU.mult),
                   deps=[s_erep], tag="R.ib1a")
    s_ib1b = P.add("dve",
                   lambda e: e.tensor_tensor(dsc[0:32, 33:34], dsc[0:32, 32:33],
                                             iotap[0:32, :], ALU.add),
                   deps=[s_ib1a], tag="R.ib1b")
    s_ci1 = P.add("pool", lambda e: e.tensor_copy(idx1[:], idxf[:, 1:2]),
                  deps=[s_i1b], tag="R.ci1")
    s_ci2 = P.add("pool", lambda e: e.tensor_copy(idx2[:], idxf[:, 3:4]),
                  deps=[s_i2b], tag="R.ci2")
    s_cib1 = P.add("pool", lambda e: e.tensor_copy(idxb1[:], dsc[0:32, 33:34]),
                   deps=[s_ib1b], tag="R.cib1")
    s_cib2 = P.add("pool", lambda e: e.tensor_copy(idxb2[:], erep[0:2, :]),
                   deps=[s_erep], tag="R.cib2")

    s_gb1 = P.add("pool",
                  lambda e: e.indirect_dma_start(
                      out=cfix[0:32, 0:128], out_offset=None,
                      in_=b1_d[:, :], in_offset=IndirectOffsetOnAxis(ap=idxb1[:], axis=0)),
                  deps=[s_cib1, ctxfin[-1]], dma=True, tag="G.b1")
    s_gb2 = P.add("pool",
                  lambda e: e.indirect_dma_start(
                      out=rowbuf[0:2, :], out_offset=None,
                      in_=b2_d[:, :], in_offset=IndirectOffsetOnAxis(ap=idxb2[:], axis=0)),
                  deps=[s_cib2, score_steps[-1]], dma=True, tag="G.b2")
    s_b1t = P.add("pe",
                  lambda e: e.transpose(PS[0:128, 0:32], cfix[0:32, 0:128],
                                        eyef[0:32, 0:32]),
                  deps=[s_gb1, s_erep, s_eyef], tag="G.b1t")
    s_b1c = P.add("dve", lambda e: e.tensor_copy(b1selT[:, :], PS[0:128, 0:32]),
                  deps=[s_b1t], tag="G.b1c")
    s_b2c = P.add("act", lambda e: e.activation(b2row[0:1, :], rowbuf[0:1, :], AF.Copy),
                  deps=[s_gb2], tag="G.b2c")

    if stage == 26:
        dumps = []
        s_dsc = P.add("act", lambda e: e.activation(x_tile[0][:, 0:64], dsc[:, :],
                                                    AF.Copy),
                      deps=[s_erep, s_i1b, s_i2b, s_ib1b, s_cib1, s_cib2, s_ci1, s_ci2],
                      tag="dump.dsc")
        dumps.append(dma("sp", out_d[0:128, 0:64], x_tile[0][:, 0:64],
                         deps=[s_dsc], tag="dump.d0", slow=True))
        s_dix = P.add("act", lambda e: e.activation(x_tile[0][:, 64:68], idxf[:, :],
                                                    AF.Copy),
                      deps=[s_dsc], tag="dump.idx")
        dumps.append(dma("sp", out_d[0:128, 64:68], x_tile[0][:, 64:68],
                         deps=[s_dix], tag="dump.d1", slow=True))
        s_der = P.add("act", lambda e: e.activation(x_tile[0][:, 68:69], erep[:, :],
                                                    AF.Copy),
                      deps=[s_dsc], tag="dump.er")
        dumps.append(dma("sp", out_d[0:128, 68:69], x_tile[0][:, 68:69],
                         deps=[s_der], tag="dump.d2", slow=True))
        P.add("sp", lambda e: None, deps=dumps, tag="final_wait")
        with nc.Block() as block:
            emctx = P.emit(nc, block)
        emctx.close()
        ctx.close()
        return nc

    g_w1 = []
    for dc in range(8):
        g_w1.append(P.add(
            "pool",
            lambda e, dc=dc: e.indirect_dma_start(
                out=w1sel[:, dc * DFF:(dc + 1) * DFF], out_offset=None,
                in_=w1_d[:, :], in_offset=IndirectOffsetOnAxis(ap=idx1[:], axis=0),
                element_offset=dc * 128 * DFF),
            deps=[s_ci1, score_steps[-1], s_b2c, s_b1c] + yt_store_steps[-8:],
            dma=True, tag=f"G.w1.{dc}"))
    g_w2 = []
    for fc in range(32):
        g_w2.append(P.add(
            "pool",
            lambda e, fc=fc: e.indirect_dma_start(
                out=w2sel[:, fc * D:(fc + 1) * D], out_offset=None,
                in_=w2_d[:, :], in_offset=IndirectOffsetOnAxis(ap=idx2[:], axis=0),
                element_offset=fc * 128 * D),
            deps=[s_ci2, score_steps[-1]] + x1_store_steps[-8:] + yt_store_steps[-8:],
            dma=True, tag=f"G.w2.{fc}"))
    if stage == 25:
        dumps = []
        s_dsc = P.add("act", lambda e: e.activation(x_tile[0][:, 0:64], dsc[:, :],
                                                    AF.Copy),
                      deps=[s_b1c, s_b2c, s_erep, s_i1b, s_i2b], tag="dump.dsc")
        dumps.append(dma("sp", out_d[0:128, 0:64], x_tile[0][:, 0:64],
                         deps=[s_dsc], tag="dump.d0", slow=True))
        s_dix = P.add("act", lambda e: e.activation(x_tile[0][:, 64:68], idxf[:, :],
                                                    AF.Copy),
                      deps=[s_dsc], tag="dump.idx")
        dumps.append(dma("sp", out_d[0:128, 64:68], x_tile[0][:, 64:68],
                         deps=[s_dix], tag="dump.d1", slow=True))
        s_w1c = P.add("act", lambda e: e.activation(x_tile[1][:, 0:1024],
                                                    w1sel[:, 0:1024], AF.Copy),
                      deps=[g_w1[-1]], tag="dump.w1c")
        dumps.append(dma("sp", out_d[128:256, :], x_tile[1][:, 0:1024],
                         deps=[s_w1c], tag="dump.d2"))
        s_w2c = P.add("act", lambda e: e.activation(x_tile[1][:, 1024:2048],
                                                    w2sel[:, 0:1024], AF.Copy),
                      deps=[g_w2[-1]], tag="dump.w2c")
        dumps.append(dma("sp", out_d[256:384, :], x_tile[1][:, 1024:2048],
                         deps=[s_w2c], tag="dump.d3"))
        s_b1d = P.add("act", lambda e: e.activation(x_tile[0][:, 68:100],
                                                    b1selT[:, :], AF.Copy),
                      deps=[s_b1c], tag="dump.b1")
        dumps.append(dma("sp", out_d[0:128, 68:100], x_tile[0][:, 68:100],
                         deps=[s_b1d], tag="dump.d4"))
        P.add("sp", lambda e: None, deps=dumps, tag="final_wait")
        with nc.Block() as block:
            emctx = P.emit(nc, block)
        emctx.close()
        ctx.close()
        return nc

    nt_c = SEQ // TG_C
    if stage == 30:
        nt_c = 1
    if stage == 31:
        nt_c = 1
    ybuf_guard = {0: [], 1: []}
    x1buf_guard = {0: [], 1: []}
    last_out = {0: None, 1: None}
    out_stores = []
    prev_hev_last = None
    for t in range(nt_c):
        buf = t % 2
        ydeps = [yt_store_steps[t]] + ybuf_guard[buf]
        if t < 2:
            ydeps += [B_last_proj, s_b1c]
        s_ldy = dma("sp", yT_C[buf].rearrange("p (dc t) -> p dc t", t=TG_C),
                    yt_scr[:, t * TG_C:(t + 1) * TG_C]
                    .rearrange("(dc p) t -> p dc t", p=128),
                    deps=ydeps, tag=f"C{t}.ldy")
        ybuf_guard[buf] = []
        xdeps = [x1_store_steps[t]] + x1buf_guard[buf]
        if t < 2:
            xdeps += [B_last_proj, B_last_qmm]
        s_ldx1 = dma("sp", x1_tile[buf].rearrange("p (s d) -> p s d", d=D),
                     x1_scr[t * TG_C:(t + 1) * TG_C, :]
                     .rearrange("(s p) d -> p s d", p=128),
                     deps=xdeps, tag=f"C{t}.ldx1")
        x1buf_guard[buf] = []

        skip_ffn2 = (stage in (31, 33))
        if stage == 32:
            dumps = []
            s_yc = P.add("dve", lambda e: e.tensor_copy(cfix[:, 0:65], eyef[:, 0:65]),
                         deps=[s_ldy, s_ldx1], tag="dump.nop")
            dumps.append(dma("sp", out_d[2048:2176, 0:65], cfix[:, 0:65],
                             deps=[s_yc], tag="dump.y0", slow=True))
            P.add("sp", lambda e: None, deps=dumps, tag="final_wait")
            with nc.Block() as block:
                emctx = P.emit(nc, block)
            emctx.close()
            ctx.close()
            return nc
        hevs = []
        w1v = w1sel.rearrange("p (dc f) -> p dc f", f=DFF)
        yv = yT_C[buf].rearrange("p (dc t) -> p dc t", t=TG_C)
        nfc = 1 if stage == 33 else 32
        for fc in range(nfc):
            pb = PX[:, (fc % 2) * 512:(fc % 2) * 512 + 256]
            mdeps = [s_ldy]
            if fc == 0:
                mdeps += g_w1
                if prev_hev_last is not None:
                    mdeps.append(prev_hev_last)
            if len(hevs) >= 2:
                mdeps.append(hevs[-2])

            def fmm(e, fc=fc, pb=pb, yv=yv):
                ins = None
                for dc in range(8):
                    ins = e.matmul(pb, w1v[:, dc, fc * 128:(fc + 1) * 128], yv[:, dc, :],
                                   start=(dc == 0), stop=(dc == 7))
                return ins
            s_f = P.add("pe", fmm, deps=mdeps, tag=f"C{t}.f1.{fc}")
            hevs.append(P.add(
                "act",
                lambda e, fc=fc, pb=pb: e.activation(h_C[:, fc * TG_C:(fc + 1) * TG_C],
                                                     pb, AF.Relu,
                                                     bias=b1selT[:, fc:fc + 1], scale=1.0),
                deps=[s_f, s_b1c], tag=f"C{t}.hev{fc}"))
        prev_hev_last = hevs[-1]

        if skip_ffn2 or stage == 33:
            break
        hv = h_C.rearrange("p (fc t) -> p fc t", t=TG_C)
        w2v = w2sel.rearrange("p (fc c) -> p fc c", c=D)
        for st in range(2):
            base = PV if st == 0 else PK
            mdeps = [hevs[-1], s_b2c, last_out[st]]
            if t == 0:
                mdeps += g_w2

            def f2mm(e, st=st, base=base):
                ins = None
                for half in range(2):
                    pb2 = base[:, half * 512:(half + 1) * 512]
                    for fc in range(32):
                        ins = e.matmul(pb2, hv[:, fc, st * 128:(st + 1) * 128],
                                       w2v[:, fc, half * 512:(half + 1) * 512],
                                       start=(fc == 0), stop=False)
                    ins = e.matmul(pb2, ones_row[0:1, :],
                                   b2row[0:1, half * 512:(half + 1) * 512],
                                   start=False, stop=True)
                return ins
            s_f2 = P.add("pe", f2mm, deps=mdeps, tag=f"C{t}.f2.{st}")

            def outc(e, st=st, base=base, buf=buf):
                e.scalar_tensor_tensor(out_C[:, st * D:st * D + 512], base[:, 0:512],
                                       1.0, x1_tile[buf][:, st * D:st * D + 512],
                                       ALU.mult, ALU.add)
                return e.scalar_tensor_tensor(out_C[:, st * D + 512:(st + 1) * D],
                                              base[:, 512:1024], 1.0,
                                              x1_tile[buf][:, st * D + 512:(st + 1) * D],
                                              ALU.mult, ALU.add)
            s_oc = P.add("dve", outc, deps=[s_f2, s_ldx1], tag=f"C{t}.out{st}")
            last_out[st] = s_oc
            x1buf_guard[buf].append(s_oc)
            ybuf_guard[buf].append(s_oc)
        s_ost = dma("act", out_d[t * TG_C:(t + 1) * TG_C, :]
                    .rearrange("(s p) d -> p s d", p=128),
                    out_C.rearrange("p (s d) -> p s d", d=D),
                    deps=[last_out[0], last_out[1]], tag=f"C{t}.ost")
        out_stores.append(s_ost)
        last_out[0] = s_ost
        last_out[1] = s_ost

    if stage in (30, 31, 33):
        dumps = []
        nch = 4 if stage == 30 else 4
        s_hc = P.add("dve", lambda e: e.tensor_copy(x_tile[0][:, 0:4096].bitcast(BF16)[:, 0:4096][: , 0:4096] if False else x_tile[0][:, 0:2048], h_C[:, 0:2048]),
                     deps=[prev_hev_last], tag="dump.h")
        dumps.append(dma("sp", out_d[1024:1152, :], x_tile[0][:, 0:1024],
                         deps=[s_hc], tag="dump.h0"))
        dumps.append(dma("sp", out_d[1152:1280, :], x_tile[0][:, 1024:2048],
                         deps=[s_hc], tag="dump.h1"))
        if stage == 30:
            dumps.append(out_stores[-1])
            dumps.append(dma("sp", out_d[1280:1408, :],
                             out_C[:, 0:1024], deps=[out_stores[-1]], tag="dump.oc"))
        P.add("sp", lambda e: None, deps=dumps, tag="final_wait")
    else:
        P.add("sp", lambda e: None, deps=out_stores[-8:], tag="final_wait")

    with nc.Block() as block:
        emctx = P.emit(nc, block)
    emctx.close()
    ctx.close()
    return nc


_NC_CACHE = {}


def _get_nc():
    stage = int(os.environ.get("ATT_STAGE", "3"))
    if "nc" not in _NC_CACHE:
        _NC_CACHE["nc"] = build_program(stage)
    return _NC_CACHE["nc"]


def _stage_inputs(L0_attn_x, L0_attn_Wq, L0_attn_bq, L0_attn_Wk, L0_attn_bk, L0_attn_Wv,
                  L0_attn_bv, L0_attn_Wproj, L0_attn_bproj, L0_attn_wq_feat,
                  L0_attn_wk_feat, L0_ffn_Wr, L0_ffn_br, L0_ffn_W1, L0_ffn_b1,
                  L0_ffn_W2, L0_ffn_b2):
    f32 = np.float32

    def c(a):
        return np.ascontiguousarray(np.asarray(a, dtype=f32))

    x = c(L0_attn_x)
    bv_full = c(L0_attn_bv)
    bvrep = np.zeros((128, H * 65), dtype=f32)
    for h in range(H):
        bvrep[:, h * 65:h * 65 + 64] = bv_full[h * 64:(h + 1) * 64][None, :]
    wqf = np.ascontiguousarray(np.concatenate([c(L0_attn_wq_feat)] * 2, axis=0))
    wkf = np.ascontiguousarray(np.concatenate([c(L0_attn_wk_feat)] * 2, axis=0))
    eyebf = np.eye(128, dtype=np.float32).astype(ml_dtypes.bfloat16)
    eyef = np.eye(128, dtype=f32)
    iotap = np.arange(128, dtype=f32).reshape(128, 1).copy()
    rowc = np.zeros((1, 16), dtype=f32)
    rowc[0, 0:8] = np.arange(8, dtype=f32) + 8.0
    common = {
        "wq": c(L0_attn_Wq), "wk": c(L0_attn_Wk), "wv": c(L0_attn_Wv),
        "wp": c(L0_attn_Wproj), "bq": c(L0_attn_bq), "bk": c(L0_attn_bk),
        "bvrep": bvrep, "bproj": c(L0_attn_bproj).reshape(1, D),
        "wqf": wqf, "wkf": wkf,
        "wrt": np.ascontiguousarray(c(L0_ffn_Wr).T),
        "br": c(L0_ffn_br).reshape(1, NEXP),
        "w1": c(L0_ffn_W1).reshape(NEXP * D, DFF),
        "b1": c(L0_ffn_b1).reshape(NEXP * 32, 128),
        "w2": c(L0_ffn_W2).reshape(NEXP * DFF, D),
        "b2": c(L0_ffn_b2).reshape(NEXP, D),
        "eyebf": eyebf, "eyef": eyef, "iotap": iotap, "rowc": rowc,
    }
    in_maps = []
    for core in range(N_CORES):
        m = dict(common)
        m["x"] = np.ascontiguousarray(x[core])
        in_maps.append(m)
    return in_maps


def kernel(**inputs):
    nc = _get_nc()
    in_maps = _stage_inputs(**inputs)
    res = run_bass_kernel_spmd(nc, in_maps, core_ids=list(range(N_CORES)))
    out = np.stack([np.asarray(r["out"], dtype=np.float32) for r in res.results], axis=0)
    return out
